# version 13
# baseline (speedup 1.0000x reference)
import numpy as np
import ml_dtypes
from contextlib import ExitStack
import concourse.bass as bass
import concourse.mybir as mybir
from concourse.bass_utils import run_bass_kernel_spmd

BF = ml_dtypes.bfloat16
F32 = mybir.dt.float32
BF16 = mybir.dt.bfloat16
I32 = mybir.dt.int32
AF = mybir.ActivationFunctionType
ALU = mybir.AluOpType
AX = mybir.AxisListType.X

NCORE = 8
SEQ = 2
T = 2048
D = 1024
H = 4
DV = 512
CH = 128
NCH = T // CH
KC = D // 128
HID = 3072
HM = HID // 128
EPS = 1e-5
MAGIC = 0x5F3759DF
Q127 = 1.0 / 127.0
LAG = 5

GAMMAS = [1.0 - 2.0 ** (-5.0 - h) for h in range(H)]

PHASES = ("ln", "attn", "wo", "ffn")
DEBUG = False
NHEADS = H
LN_STAGE = 7


def WG(eng, sem_h, val):
    if val > 0:
        eng.wait_ge(sem_h, val)


class Sem:
    def __init__(self, nc, name):
        self.h = nc.semaphore(name).__enter__()
        self.n = 0

    def inc(self, ins, k=1):
        ins.then_inc(self.h, k)
        self.n += k
        return self.n


def _host_consts():
    f = np.arange(128, dtype=np.float64)
    inv_freq = 1.0 / (10000.0 ** (2.0 * f / 256.0))
    t = np.arange(T, dtype=np.float64)
    ang = np.outer(inv_freq, t)
    cosb = np.cos(ang).astype(BF)
    sinb = np.sin(ang).astype(BF)
    tt = np.arange(512) % 128
    cq = np.zeros((128, H * 512), dtype=np.float64)
    for h in range(H):
        cq[:, h * 512:(h + 1) * 512] = (GAMMAS[h] ** (tt + 1.0)) / 16.0
    cqb = cq.astype(BF)
    j = np.arange(128)[:, None]
    i = np.arange(128)[None, :]
    dp = np.zeros((128, H * 128), dtype=np.float64)
    for h in range(H):
        dp[:, h * 128:(h + 1) * 128] = np.where(i >= j, GAMMAS[h] ** (-(j + 1.0)), 0.0)
    dpb = dp.astype(BF)
    ck = np.zeros((128, H), dtype=np.float64)
    for h in range(H):
        ck[:, h] = GAMMAS[h] ** (127.0 - np.arange(128))
    ckv = ck.astype(np.float32)
    idn = np.eye(128, dtype=BF)
    return cosb, sinb, cqb, dpb, ckv, idn


def _quake_rsqrt(v, out_ap, in_ap, qi, qf, sem=None):
    n = in_ap.shape[-1]
    v.tensor_single_scalar(out=qi[:, 0:n], in_=in_ap.bitcast(I32),
                           scalar=1, op=ALU.arith_shift_right)
    v.drain()
    v.tensor_scalar(out=qi[:, 0:n], in0=qi[:, 0:n], scalar1=MAGIC, scalar2=-1,
                    op0=ALU.subtract, op1=ALU.mult)
    v.drain()
    y = qi.ap().bitcast(F32)[:, 0:n]
    v.tensor_mul(out=qf[:, 0:n], in0=y, in1=y)
    v.drain()
    v.tensor_mul(out=qf[:, 0:n], in0=qf[:, 0:n], in1=in_ap)
    v.drain()
    v.tensor_scalar(out=qf[:, 0:n], in0=qf[:, 0:n], scalar1=-0.5, scalar2=1.5,
                    op0=ALU.mult, op1=ALU.add)
    v.drain()
    v.tensor_mul(out=out_ap, in0=y, in1=qf[:, 0:n])
    v.drain()
    v.tensor_mul(out=qf[:, 0:n], in0=out_ap, in1=out_ap)
    v.drain()
    v.tensor_mul(out=qf[:, 0:n], in0=qf[:, 0:n], in1=in_ap)
    v.drain()
    v.tensor_scalar(out=qf[:, 0:n], in0=qf[:, 0:n], scalar1=-0.5, scalar2=1.5,
                    op0=ALU.mult, op1=ALU.add)
    v.drain()
    ins = v.tensor_mul(out=out_ap, in0=out_ap, in1=qf[:, 0:n])
    if sem is not None:
        sem.inc(ins)
    v.drain()


def build_nc():
    nc = bass.Bass("TRN2", target_bir_lowering=False)

    x_d = nc.dram_tensor("x", [SEQ * T, D], F32, kind="ExternalInput")
    wq_d = nc.dram_tensor("wq", [D, D], F32, kind="ExternalInput")
    wk_d = nc.dram_tensor("wk", [D, D], F32, kind="ExternalInput")
    wv_d = nc.dram_tensor("wv", [D, 2048], F32, kind="ExternalInput")
    wgt_d = nc.dram_tensor("wgt", [D, 2048], F32, kind="ExternalInput")
    wo_d = nc.dram_tensor("wo", [2048, D], F32, kind="ExternalInput")
    gww_d = nc.dram_tensor("gww", [128, 2048], F32, kind="ExternalInput")
    fgw_d = nc.dram_tensor("fgw", [HM, D, 128], I32, kind="ExternalInput")
    fuw_d = nc.dram_tensor("fuw", [HM, D, 128], I32, kind="ExternalInput")
    fdw_d = nc.dram_tensor("fdw", [HID, D], I32, kind="ExternalInput")
    cos_d = nc.dram_tensor("cosb", [128, T], BF16, kind="ExternalInput")
    sin_d = nc.dram_tensor("sinb", [128, T], BF16, kind="ExternalInput")
    cq_d = nc.dram_tensor("cqb", [128, H * 512], BF16, kind="ExternalInput")
    dp_d = nc.dram_tensor("dpb", [128, H * 128], BF16, kind="ExternalInput")
    ck_d = nc.dram_tensor("ckv", [128, H], F32, kind="ExternalInput")
    id_d = nc.dram_tensor("idn", [128, 128], BF16, kind="ExternalInput")

    spill_kind = "ExternalOutput" if DEBUG else "Internal"
    og_sp = nc.dram_tensor("og_spill", [H * NCH, 128, 512], BF16, kind=spill_kind)
    s_sp = nc.dram_tensor("s_spill", [T, D], BF16, kind=spill_kind)
    out_d = nc.dram_tensor("out", [SEQ * T, D], F32, kind="ExternalOutput")
    dbg_xnT = (nc.dram_tensor("dbg_xnT", [128, KC * T], BF16, kind="ExternalOutput")
               if DEBUG else None)

    A = nc.alloc_sbuf_tensor
    cos_sb = A("cos_sb", [128, T], BF16)
    sin_sb = A("sin_sb", [128, T], BF16)
    cq_sb = A("cq_sb", [128, H * 512], BF16)
    dp_sb = A("dp_sb", [128, H * 128], BF16)
    ck_sb = A("ck_sb", [128, H], F32)
    id_sb = A("id_sb", [128, 128], BF16)
    gw_sb = A("gw_sb", [128, 2048], BF16)
    xnT = A("xnT", [128, KC * T], BF16)

    S0 = A("S0", [128, 1024], BF16)
    S1 = A("S1", [128, 1024], BF16)
    Sb = [S0, S1]
    ss_sb = A("ss_sb", [128, 8], F32)
    rr_sb = A("rr_sb", [128, 8], F32)
    qk1 = A("qk1", [128, 8], I32)
    qk2 = A("qk2", [128, 8], F32)
    qk3 = A("qk3", [128, 8], F32)
    lm_sb = A("lm_sb", [128, 8], F32)
    lr_sb = A("lr_sb", [128, 4], F32)
    lb_sb = A("lb_sb", [128, 4], F32)

    sems = {n: Sem(nc, n) for n in
            ("sWL sWQK sWO2 sWD sWH sLNq sLNv sLNs sTln sLNe "
             "sTp sPE sRP sGT sTa sTk sTo sTu sTog "
             "sAT sS sRr sOT sOGv sKT sOE sOGE "
             "sOL0 sOL1 sTwo sWoE sTd sTg sTup sFE sHT sFE2 sCv sCL "
             "sXL0 sXL1 sXL2 sXL3 sSP0 sSP1 sSS0 sSS1 "
             "sFW0 sFW1 sXF0 sXF1 sOUT0 sOUT1").split()}
    g = dict(sems)
    g.update(dict(nc=nc, x_d=x_d, wq_d=wq_d, wk_d=wk_d, wv_d=wv_d, wgt_d=wgt_d,
                  wo_d=wo_d, fgw_d=fgw_d, fuw_d=fuw_d, fdw_d=fdw_d,
                  og_sp=og_sp, s_sp=s_sp, out_d=out_d,
                  cos_sb=cos_sb, sin_sb=sin_sb, cq_sb=cq_sb, dp_sb=dp_sb,
                  ck_sb=ck_sb, id_sb=id_sb, gw_sb=gw_sb, xnT=xnT, Sb=Sb,
                  ss_sb=ss_sb, rr_sb=rr_sb, qk1=qk1, qk2=qk2, qk3=qk3,
                  lm_sb=lm_sb, lr_sb=lr_sb, lb_sb=lb_sb))
    G = type("G", (), g)

    with nc.sbuf_tensor("gwf_sb", [128, 2048], F32) as gwf_sb, nc.Block() as blk:
        @blk.sync
        def _(sy):
            for dst, src in ((cos_sb, cos_d), (sin_sb, sin_d), (cq_sb, cq_d),
                             (dp_sb, dp_d), (ck_sb, ck_d), (id_sb, id_d),
                             (gwf_sb, gww_d)):
                sy.dma_start(out=dst[:, :], in_=src[:, :]).then_inc(G.sCL.h, 16)
            G.sCL.n += 7 * 16

        @blk.vector
        def _(v):
            WG(v, G.sCL.h, G.sCL.n)
            v.tensor_copy(out=gw_sb[:, :], in_=gwf_sb[:, :])

    nseq = 1 if DEBUG else SEQ
    for s in range(nseq):
        _phase_attn(G, s)
        _phase_wo_ffn(G, s)
    if DEBUG:
        with nc.Block() as blk:
            @blk.sync
            def _(sy):
                sy.dma_start(out=dbg_xnT[:, :], in_=xnT[:, :]).then_inc(G.sCL.h, 16)
                G.sCL.n += 16
                WG(sy, G.sCL.h, G.sCL.n)

    with nc.Block() as blk:
        @blk.sync
        def _(sy):
            WG(sy, G.sOUT0.h, G.sOUT0.n)
            WG(sy, G.sOUT1.h, G.sOUT1.n)

    return nc


def _emit_qk_evac(G, sc, b, eraw, U_ps, pair):
    WG(sc, G.sRP.h, b["rp"] + pair - 1)
    for d in range(2):
        grp = 2 * pair + d
        WG(sc, G.sTp.h, b["tp"] + grp + 1)
        par = grp % 2
        gp = pair % 2
        sc.copy(out=eraw[:, gp * 1024 + d * 512:gp * 1024 + (d + 1) * 512],
                in_=U_ps[:, par * 512:(par + 1) * 512]).then_inc(G.sPE.h, 1)
        G.sPE.n += 1


def _emit_ln(G, s, blk, x_sb, scr, xn_sb, lnps, extra_cb=None):
    nc = G.nc
    sLNq, sLNv, sLNs, sTln, sLNe = G.sLNq, G.sLNv, G.sLNs, G.sTln, G.sLNe
    sXLs = [G.sXL0, G.sXL1, G.sXL2, G.sXL3]
    x0s = [t.n for t in sXLs]
    q0, v0, s0, t0, e0 = sLNq.n, sLNv.n, sLNs.n, sTln.n, sLNe.n

    @blk.sync
    def _(sy):
        for tc in range(NCH):
            WG(sy, sLNs.h, s0 + tc - 3)
            sl = sXLs[tc % 4]
            sy.dma_start(out=x_sb[:, (tc % 4) * D:(tc % 4 + 1) * D],
                         in_=G.x_d[s * T + tc * 128: s * T + (tc + 1) * 128, :]
                         ).then_inc(sl.h, 16)
            sl.n += 16

    @blk.vector
    def _(v):
        for w in range(4):
            for jj in range(4):
                tc = 4 * w + jj
                WG(v, sXLs[tc % 4].h, x0s[tc % 4] + 16 * (tc // 4 + 1))
                v.reduce_sum(out=G.lm_sb[:, jj:jj + 1],
                             in_=x_sb[:, (tc % 4) * D:(tc % 4 + 1) * D], axis=AX)
            WG(v, sLNq.h, q0 + 4 * (w + 1))
            WG(v, sLNs.h, s0 + 4 * w)
            v.drain()
            m1 = G.lm_sb[:, 0:4]
            m2 = G.lm_sb[:, 4:8]
            v.tensor_scalar_mul(out=G.qk2[:, 0:4], in0=m1, scalar1=1.0 / D)
            v.tensor_scalar(out=G.qk2[:, 4:8], in0=m2, scalar1=1.0 / D, scalar2=EPS,
                            op0=ALU.mult, op1=ALU.add)
            v.drain()
            v.tensor_mul(out=G.qk3[:, 0:4], in0=G.qk2[:, 0:4], in1=G.qk2[:, 0:4])
            v.drain()
            v.tensor_sub(out=G.qk3[:, 4:8], in0=G.qk2[:, 4:8], in1=G.qk3[:, 0:4])
            v.drain()
            _quake_rsqrt(v, G.lr_sb[:, 0:4], G.qk3[:, 4:8], G.qk1, G.qk2)
            ins = v.scalar_tensor_tensor(out=G.lb_sb[:, 0:4], in0=G.lm_sb[:, 0:4],
                                         scalar=-1.0 / D, in1=G.lr_sb[:, 0:4],
                                         op0=ALU.mult, op1=ALU.mult)
            sLNv.inc(ins)

    @blk.scalar
    def _(sc):
        def evac(tc):
            WG(sc, sTln.h, t0 + tc + 1)
            for k in range(KC):
                ins = sc.copy(
                    out=G.xnT[:, k * T + tc * 128: k * T + (tc + 1) * 128],
                    in_=lnps[:, k * 128:(k + 1) * 128])
            ins.then_inc(sLNe.h, 1)
            sLNe.n += 1

        for w in range(4):
            for jj in range(4):
                tc = 4 * w + jj
                WG(sc, sXLs[tc % 4].h, x0s[tc % 4] + 16 * (tc // 4 + 1))
                WG(sc, sLNv.h, v0 + w)
                sc.activation(out=scr[:, :],
                              in_=x_sb[:, (tc % 4) * D:(tc % 4 + 1) * D],
                              func=AF.Square,
                              accum_out=G.lm_sb[:, 4 + jj:5 + jj]
                              ).then_inc(sLNq.h, 1)
                sLNq.n += 1
                sc.drain()
            if w >= 1:
                for jj in range(4):
                    evac(4 * (w - 1) + jj)
                if extra_cb is not None:
                    extra_cb(sc, w)
            for jj in range(4):
                tc = 4 * w + jj
                WG(sc, sLNv.h, v0 + w + 1)
                WG(sc, sTln.h, t0 + tc - 3)
                sc.activation(out=xn_sb[:, (tc % 4) * D:(tc % 4 + 1) * D],
                              in_=x_sb[:, (tc % 4) * D:(tc % 4 + 1) * D],
                              func=AF.Identity,
                              bias=G.lb_sb[:, jj:jj + 1],
                              scale=G.lr_sb[:, jj:jj + 1]).then_inc(sLNs.h, 1)
                sLNs.n += 1
        for tc in range(12, NCH):
            evac(tc)

    def ln_tr(te, tc):
        WG(te, sLNs.h, s0 + tc + 1)
        WG(te, sLNe.h, e0 + tc)
        for k in range(KC):
            ins = te.transpose(
                lnps[:, k * 128:(k + 1) * 128],
                xn_sb[:, (tc % 4) * D + k * 128:(tc % 4) * D + (k + 1) * 128],
                G.id_sb[:, :])
        ins.then_inc(G.sTln.h, 1)
        G.sTln.n += 1

    return e0, ln_tr


def _phase_attn(G, s):
    nc = G.nc
    (sWL, sWH, sTp, sPE, sRP, sGT, sTa, sTk, sTo, sTu, sTog, sAT, sS, sRr,
     sOT, sOGv, sKT, sOE, sOGE) = (G.sWL, G.sWH, G.sTp, G.sPE, G.sRP, G.sGT,
                                   G.sTa, G.sTk, G.sTo, G.sTu, G.sTog, G.sAT,
                                   G.sS, G.sRr, G.sOT, G.sOGv, G.sKT, G.sOE,
                                   G.sOGE)
    sSPs = [G.sSP0, G.sSP1]
    sLNe = G.sLNe
    sWQK = G.sWQK
    Sb = G.Sb
    es = ExitStack()
    SB = lambda n, sh, dt: es.enter_context(nc.sbuf_tensor(f"{n}{s}", sh, dt))
    PS = lambda n, sh, dt: es.enter_context(nc.psum_tensor(f"{n}{s}", sh, dt))
    wq_sb = SB("wq_sb", [128, KC * D], BF16)
    wk_sb = SB("wk_sb", [128, KC * D], BF16)
    wv_sb = SB("wv_sb", [128, KC * 512], BF16)
    wg_sb = SB("wg_sb", [128, KC * 512], BF16)
    qT = SB("qT", [128, 2 * T], BF16)
    kT = SB("kT", [128, 2 * T], BF16)
    vtb = SB("vtb", [128, NCH * 512], BF16)
    ggb = SB("ggb", [128, NCH * 512], BF16)
    ropem = SB("rope_m", [128, 6 * 512], BF16)
    eraw = SB("eraw", [128, 4 * 512], BF16)
    gtmp = SB("gtmp", [128, 2 * 512], BF16)
    atm = SB("atm", [128, 2 * 128], BF16)
    ktok = SB("ktok", [128, 2 * 256], BF16)
    o_win = SB("o_win", [128, 6 * 512], BF16)
    ot8 = SB("ot8", [128, 4 * 512], BF16)
    og_sb = SB("og_sb", [128, 4 * 512], BF16)
    og_st = SB("og_st", [128, 2 * 512], BF16)
    x_sb = SB("ln_x", [128, 4 * D], F32)
    scr = SB("ln_scr", [128, D], F32)
    xn_sb = SB("ln_xn", [128, 4 * D], BF16)
    U_ps = PS("U_ps", [128, 1024], F32)
    o_ps = PS("o_ps", [128, 2 * 512], F32)
    at_ps = PS("at_ps", [128, 2 * 128], F32)
    kt_ps = PS("kt_ps", [128, 2 * 256], BF16)
    ogT_ps = PS("ogT_ps", [128, 512], BF16)
    lnps = PS("lnps", [128, D], BF16)
    with es, nc.Block() as blk:
        wl0 = sWQK.n
        b_h0 = dict(tp=sTp.n, rp=sRP.n, pe=sPE.n)

        def _ln_extra(sc, w):
            _emit_qk_evac(G, sc, b_h0, eraw, U_ps, w - 1)

        ln_e0, ln_tr = _emit_ln(G, s, blk, x_sb, scr, xn_sb, lnps, _ln_extra)

        @blk.gpsimd
        def _(g):
            for k in range(KC):
                g.dma_start(out=wq_sb[:, k * D:(k + 1) * D],
                            in_=G.wq_d[k * 128:(k + 1) * 128, :]).then_inc(sWQK.h, 16)
                g.dma_start(out=wk_sb[:, k * D:(k + 1) * D],
                            in_=G.wk_d[k * 128:(k + 1) * 128, :]).then_inc(sWQK.h, 16)
            sWQK.n += 16 * 2 * KC

        def emit_og_gp(g, bo):
            for j in range(NCH):
                WG(g, sOT.h, bo["ot"] + j + 1)
                WG(g, sGT.h, bo["gt"] + j + 1)
                WG(g, sTog.h, bo["tog"] + j - 3)
                ins = g.tensor_mul(
                    out=og_sb[:, (j % 4) * 512:(j % 4 + 1) * 512],
                    in0=ot8[:, (j % 4) * 512:(j % 4 + 1) * 512],
                    in1=ggb[:, j * 512:(j + 1) * 512])
                sOGv.inc(ins)

        bprev = None
        for h in range(NHEADS):
            tp_gate = sTp.n

            @blk.gpsimd
            def _(g, h=h, tp_gate=tp_gate, bprev=bprev):
                WG(g, sTp.h, tp_gate)
                for k in range(KC):
                    g.dma_start(out=wv_sb[:, k * 512:(k + 1) * 512],
                                in_=G.wv_d[k * 128:(k + 1) * 128, h * 512:(h + 1) * 512]
                                ).then_inc(sWH.h, 16)
                    g.dma_start(out=wg_sb[:, k * 512:(k + 1) * 512],
                                in_=G.wgt_d[k * 128:(k + 1) * 128, h * 512:(h + 1) * 512]
                                ).then_inc(sWH.h, 16)
                sWH.n += 16 * 2 * KC
                if bprev is not None:
                    emit_og_gp(g, bprev)
            wl_h = sWH.n

            b = dict(tp=sTp.n, pe=(b_h0["pe"] if h == 0 else sPE.n), rp=sRP.n,
                     gt=sGT.n,
                     ta=sTa.n, tk=sTk.n, to=sTo.n, tu=sTu.n, tog=sTog.n,
                     at=sAT.n, st=sS.n, rr=sRr.n, ot=sOT.n, ogv=sOGv.n,
                     kt=sKT.n, oe=sOE.n, oge=sOGE.n,
                     sp=[sSPs[0].n, sSPs[1].n])

            @blk.tensor
            def _(te, h=h, b=b, wl_h=wl_h):
                if h == 0:
                    WG(te, sWQK.h, wl0 + 16 * 2 * KC)
                WG(te, sS.h, b["st"])
                grp = 0
                for wsb in (wq_sb, wk_sb):
                    for n in range(4):
                        if h == 0 and wsb is wq_sb:
                            for tc in range(4 * n, 4 * n + 4):
                                ln_tr(te, tc)
                        if h == 0:
                            WG(te, sLNe.h, ln_e0 + 4 * (n + 1))
                        for d in range(2):
                            par = grp % 2
                            WG(te, sPE.h, b["pe"] + grp - 1)
                            for k in range(KC):
                                ins = te.matmul(
                                    U_ps[:, par * 512:(par + 1) * 512],
                                    wsb[:, k * D + (2 * h + d) * 128:
                                        k * D + (2 * h + d + 1) * 128],
                                    G.xnT[:, k * T + n * 512:k * T + (n + 1) * 512],
                                    start=(k == 0), stop=(k == KC - 1))
                            sTp.inc(ins)
                            grp += 1
                WG(te, sWH.h, wl_h)
                for wsb in (wv_sb, wg_sb):
                    for c in range(NCH):
                        par = grp % 2
                        WG(te, sPE.h, b["pe"] + grp - 1)
                        for k in range(KC):
                            ins = te.matmul(
                                U_ps[:, par * 512:(par + 1) * 512],
                                G.xnT[:, k * T + c * 128:k * T + (c + 1) * 128],
                                wsb[:, k * 512:(k + 1) * 512],
                                start=(k == 0), stop=(k == KC - 1))
                        sTp.inc(ins)
                        grp += 1

            @blk.scalar
            def _(sc, h=h, b=b):
                WG(sc, sTo.h, b["to"])
                WG(sc, sTu.h, b["tu"])
                first_pair = 3 if h == 0 else 0
                for pair in range(first_pair, 8):
                    _emit_qk_evac(G, sc, b, eraw, U_ps, pair)
                grp = 16
                for c in range(NCH):
                    WG(sc, sTp.h, b["tp"] + grp + 1)
                    par = grp % 2
                    sc.copy(out=vtb[:, c * 512:(c + 1) * 512],
                            in_=U_ps[:, par * 512:(par + 1) * 512]).then_inc(sPE.h, 1)
                    sPE.n += 1
                    grp += 1
                for c in range(NCH):
                    WG(sc, sTp.h, b["tp"] + grp + 1)
                    WG(sc, sGT.h, b["gt"] + c - 1)
                    par = grp % 2
                    sc.activation(out=gtmp[:, (c % 2) * 512:(c % 2 + 1) * 512],
                                  in_=U_ps[:, par * 512:(par + 1) * 512],
                                  func=AF.Silu).then_inc(sPE.h, 1)
                    sPE.n += 1
                    grp += 1

            @blk.vector
            def _(v, h=h, b=b):
                WG(v, sTo.h, b["to"])
                WG(v, sTa.h, b["ta"])
                WG(v, sTk.h, b["tk"])
                v.memset(Sb[0][:, :], 0.0)
                for mi, (dst, has_cq) in enumerate(((qT, True), (kT, False))):
                    for n in range(4):
                        pair = mi * 4 + n
                        gp = pair % 2
                        WG(v, sPE.h, b["pe"] + 2 * pair + 2)
                        v.drain()
                        e0r = eraw[:, gp * 1024:gp * 1024 + 512]
                        e1r = eraw[:, gp * 1024 + 512:gp * 1024 + 1024]
                        e0 = dst[:, 0 * T + n * 512:0 * T + (n + 1) * 512]
                        e1 = dst[:, 1 * T + n * 512:1 * T + (n + 1) * 512]
                        c_sl = G.cos_sb[:, n * 512:(n + 1) * 512]
                        s_sl = G.sin_sb[:, n * 512:(n + 1) * 512]
                        p0 = ropem[:, 0:512]
                        p1 = ropem[:, 512:1024]
                        p2 = ropem[:, 1024:1536]
                        p3 = ropem[:, 1536:2048]
                        if has_cq:
                            cqs = G.cq_sb[:, h * 512:(h + 1) * 512]
                            a0 = ropem[:, 2048:2560]
                            a1 = ropem[:, 2560:3072]
                            v.tensor_mul(out=a0, in0=e0r, in1=cqs)
                            v.tensor_mul(out=a1, in0=e1r, in1=cqs)
                            v.drain()
                            e0r, e1r = a0, a1
                        v.tensor_mul(out=p0, in0=e0r, in1=c_sl)
                        v.tensor_mul(out=p1, in0=e1r, in1=s_sl)
                        v.tensor_mul(out=p2, in0=e0r, in1=s_sl)
                        v.tensor_mul(out=p3, in0=e1r, in1=c_sl)
                        v.drain()
                        v.tensor_sub(out=e0, in0=p0, in1=p1)
                        ins = v.tensor_add(out=e1, in0=p3, in1=p2)
                        sRP.inc(ins)
                for c in range(NCH):
                    WG(v, sPE.h, b["pe"] + 32 + c + 1)
                    WG(v, sOGv.h, b["ogv"] + c - 15)
                    ins = v.tensor_mul(out=ggb[:, c * 512:(c + 1) * 512],
                                       in0=gtmp[:, (c % 2) * 512:(c % 2 + 1) * 512],
                                       in1=G.gw_sb[:, h * 512:(h + 1) * 512])
                    sGT.inc(ins)

            @blk.tensor
            def _(te, h=h, b=b):
                WG(te, sRP.h, b["rp"] + 8)
                WG(te, sPE.h, b["pe"] + 32 + NCH)

                def og_tr(c):
                    WG(te, sOGv.h, b["ogv"] + c + 1)
                    WG(te, sOGE.h, b["oge"] + c)
                    for jj in range(4):
                        ins = te.transpose(
                            ogT_ps[:, jj * 128:(jj + 1) * 128],
                            og_sb[:, (c % 4) * 512 + jj * 128:
                                  (c % 4) * 512 + (jj + 1) * 128],
                            G.id_sb[:, :])
                    sTog.inc(ins)

                def at_kt(cc):
                    par2 = cc % 2
                    WG(te, sAT.h, b["at"] + cc)
                    for d in range(2):
                        ins = te.matmul(at_ps[:, par2 * 128:(par2 + 1) * 128],
                                        kT[:, d * T + cc * 128:d * T + (cc + 1) * 128],
                                        qT[:, d * T + cc * 128:d * T + (cc + 1) * 128],
                                        start=(d == 0), stop=(d == 1))
                    sTa.inc(ins)
                    WG(te, sKT.h, b["kt"] + cc)
                    for d in range(2):
                        ins = te.transpose(
                            kt_ps[:, par2 * 256 + d * 128:par2 * 256 + (d + 1) * 128],
                            kT[:, d * T + cc * 128:d * T + (cc + 1) * 128],
                            G.id_sb[:, :])
                    sTk.inc(ins)

                at_kt(0)
                at_kt(1)
                for c in range(NCH):
                    par = c % 2
                    if c >= LAG:
                        og_tr(c - LAG)
                    WG(te, sAT.h, b["at"] + c + 1)
                    WG(te, sOE.h, b["oe"] + 2 * c - 2)
                    WG(te, sS.h, b["st"] + 2 * c)
                    ins = te.matmul(o_ps[:, par * 512:(par + 1) * 512],
                                    atm[:, par * 128:(par + 1) * 128],
                                    vtb[:, c * 512:(c + 1) * 512],
                                    start=True, stop=(c == 0))
                    if c > 0:
                        for d in range(2):
                            ins = te.matmul(o_ps[:, par * 512:(par + 1) * 512],
                                            qT[:, d * T + c * 128:d * T + (c + 1) * 128],
                                            Sb[c % 2][:, d * 512:(d + 1) * 512],
                                            start=False, stop=(d == 1))
                    sTo.inc(ins)
                    WG(te, sKT.h, b["kt"] + c + 1)
                    for d in range(2):
                        WG(te, sS.h, b["st"] + 2 * c + d - 1)
                        ins = te.matmul(U_ps[:, d * 512:(d + 1) * 512],
                                        ktok[:, par * 256 + d * 128:
                                             par * 256 + (d + 1) * 128],
                                        vtb[:, c * 512:(c + 1) * 512],
                                        start=True, stop=True)
                        sTu.inc(ins)
                    if c + 2 < NCH:
                        at_kt(c + 2)
                for c in range(NCH - LAG, NCH):
                    og_tr(c)

            @blk.vector
            def _(v, h=h, b=b):
                def mask(cc):
                    par2 = cc % 2
                    WG(v, sTa.h, b["ta"] + cc + 1)
                    WG(v, sTo.h, b["to"] + cc - 1)
                    ins = v.tensor_mul(out=atm[:, par2 * 128:(par2 + 1) * 128],
                                       in0=at_ps[:, par2 * 128:(par2 + 1) * 128],
                                       in1=G.dp_sb[:, h * 128:(h + 1) * 128])
                    sAT.inc(ins)

                mask(0)
                mask(1)
                for c in range(NCH):
                    w = c // 4
                    if c < NCH - 1:
                        for d in range(2):
                            WG(v, sTu.h, b["tu"] + 2 * c + d + 1)
                            ins = v.scalar_tensor_tensor(
                                out=Sb[(c + 1) % 2][:, d * 512:(d + 1) * 512],
                                in0=Sb[c % 2][:, d * 512:(d + 1) * 512],
                                scalar=float(GAMMAS[h] ** CH),
                                in1=U_ps[:, d * 512:(d + 1) * 512],
                                op0=ALU.mult, op1=ALU.add)
                            sS.inc(ins)
                    if c + 2 < NCH:
                        mask(c + 2)
                    if c % 4 == 3:
                        WG(v, sOE.h, b["oe"] + 8 * w + 8)
                        WG(v, sOT.h, b["ot"] + 4 * w - 4)
                        wp = w % 2
                        v.tensor_scalar(out=G.qk3[:, 0:4],
                                        in0=G.ss_sb[:, wp * 4:wp * 4 + 4],
                                        scalar1=1.0 / 512.0, scalar2=EPS,
                                        op0=ALU.mult, op1=ALU.add)
                        v.drain()
                        _quake_rsqrt(v, G.rr_sb[:, wp * 4:wp * 4 + 4],
                                     G.qk3[:, 0:4], G.qk1, G.qk2, sem=sRr)

            @blk.scalar
            def _(sc, h=h, b=b):
                def og_evac(c):
                    WG(sc, sTog.h, b["tog"] + c + 1)
                    WG(sc, sSPs[c % 2].h, b["sp"][c % 2] + 16 * (c // 2))
                    sc.copy(out=og_st[:, (c % 2) * 512:(c % 2 + 1) * 512],
                            in_=ogT_ps[:, :]).then_inc(sOGE.h, 1)
                    sOGE.n += 1

                def ot_batch(wb):
                    WG(sc, sRr.h, b["rr"] + wb + 1)
                    for cc in range(4):
                        j = 4 * wb + cc
                        WG(sc, sOGv.h, b["ogv"] + j - 3)
                        sc.activation(out=ot8[:, (j % 4) * 512:(j % 4 + 1) * 512],
                                      in_=o_win[:, (j % 6) * 512:(j % 6 + 1) * 512],
                                      func=AF.Identity,
                                      scale=G.rr_sb[:, (wb % 2) * 4 + cc:
                                                    (wb % 2) * 4 + cc + 1]
                                      ).then_inc(sOT.h, 1)
                        sOT.n += 1

                for c in range(NCH):
                    par = c % 2
                    w = c // 4
                    if c % 4 == 1 and c >= 5:
                        ot_batch(w - 1)
                    if c >= LAG + 1:
                        og_evac(c - LAG - 1)
                    WG(sc, sTk.h, b["tk"] + c + 1)
                    WG(sc, sTu.h, b["tu"] + 2 * c - 2)
                    sc.activation(out=ktok[:, par * 256:(par + 1) * 256],
                                  in_=kt_ps[:, par * 256:(par + 1) * 256],
                                  func=AF.Identity,
                                  scale=G.ck_sb[:, h:h + 1]).then_inc(sKT.h, 1)
                    sKT.n += 1
                    WG(sc, sTo.h, b["to"] + c + 1)
                    sc.copy(out=o_win[:, (c % 6) * 512:(c % 6 + 1) * 512],
                            in_=o_ps[:, par * 512:(par + 1) * 512]).then_inc(sOE.h, 1)
                    sOE.n += 1
                    WG(sc, sRr.h, b["rr"] + w - 1)
                    sc.activation(out=scr[:, 0:512],
                                  in_=o_ps[:, par * 512:(par + 1) * 512],
                                  func=AF.Square,
                                  accum_out=G.ss_sb[:, (w % 2) * 4 + (c % 4):
                                                    (w % 2) * 4 + (c % 4) + 1]
                                  ).then_inc(sOE.h, 1)
                    sOE.n += 1
                ot_batch(NCH // 4 - 1)
                for c in range(NCH - LAG - 1, NCH):
                    og_evac(c)

            @blk.sync
            def _(sy, h=h, b=b):
                for c in range(NCH):
                    WG(sy, sOGE.h, b["oge"] + c + 1)
                    sl = sSPs[c % 2]
                    sy.dma_start(out=G.og_sp[h * NCH + c, :, :],
                                 in_=og_st[:, (c % 2) * 512:(c % 2 + 1) * 512]
                                 ).then_inc(sl.h, 16)
                    sl.n += 16
                if h == H - 1:
                    WG(sy, sSPs[0].h, sSPs[0].n)
                    WG(sy, sSPs[1].h, sSPs[1].n)

            bprev = b

        @blk.gpsimd
        def _(g, bprev=bprev):
            emit_og_gp(g, bprev)


def _phase_wo_ffn(G, s):
    nc = G.nc
    sWO2, sWD, sTwo, sWoE = G.sWO2, G.sWD, G.sTwo, G.sWoE
    sOLs = [G.sOL0, G.sOL1]
    sSSs = [G.sSS0, G.sSS1]
    (sTg, sTup, sFE, sHT, sTd, sFE2, sCv) = (
        G.sTg, G.sTup, G.sFE, G.sHT, G.sTd, G.sFE2, G.sCv)
    sFWs = [G.sFW0, G.sFW1]
    sXFs = [G.sXF0, G.sXF1]
    sOUTs = [G.sOUT0, G.sOUT1]

    outer = ExitStack()
    SBo = lambda n, sh, dt: outer.enter_context(nc.sbuf_tensor(f"{n}{s}", sh, dt))
    PSo = lambda n, sh, dt: outer.enter_context(nc.psum_tensor(f"{n}{s}", sh, dt))
    with outer:
        wd_sb = SBo("wd_sb", [128, HM * D], BF16)
        wgu = SBo("wgu", [128, 4 * 1024], BF16)
        P0 = PSo("P0", [128, D], F32)
        P1 = PSo("P1", [128, D], F32)
        P2 = PSo("P2", [128, D], F32)
        P3 = PSo("P3", [128, D], F32)

        fw_pre = [sFWs[0].n, sFWs[1].n]

        inner1 = ExitStack()
        with inner1:
            wo_sb = inner1.enter_context(nc.sbuf_tensor(f"wo_sb{s}", [128, 16 * D], BF16))
            ogl = inner1.enter_context(nc.sbuf_tensor(f"ogl{s}", [128, 2 * 2048], BF16))
            s_st = inner1.enter_context(nc.sbuf_tensor(f"s_st{s}", [128, 2 * D], BF16))
            with nc.Block() as blk:
                wopss = (P0, P1)
                wl0, two0, we0 = sWO2.n, sTwo.n, sWoE.n
                ol0 = [sOLs[0].n, sOLs[1].n]
                ss0 = [sSSs[0].n, sSSs[1].n]

                @blk.gpsimd
                def _(g):
                    for k in range(16):
                        g.dma_start(out=wo_sb[:, k * D:(k + 1) * D],
                                    in_=G.wo_d[k * 128:(k + 1) * 128, :]).then_inc(sWO2.h, 16)
                    sWO2.n += 16 * 16
                    for k in range(HM):
                        g.dma_start(out=wd_sb[:, k * D:(k + 1) * D],
                                    in_=G.fdw_d[k * 128:(k + 1) * 128, :]).then_inc(sWD.h, 16)
                    sWD.n += 16 * HM
                    for m in range(2):
                        WG(g, sTup.h, G.sTup.n + m - 1)
                        par = m % 2
                        sl = sFWs[par]
                        g.dma_start(out=wgu[:, par * 1024:(par + 1) * 1024]
                                    .rearrange("p (k j) -> p k j", j=128),
                                    in_=G.fgw_d[m].rearrange("(k p) j -> p k j", p=128)
                                    ).then_inc(sl.h, 16)
                        g.dma_start(out=wgu[:, 2048 + par * 1024:2048 + (par + 1) * 1024]
                                    .rearrange("p (k j) -> p k j", j=128),
                                    in_=G.fuw_d[m].rearrange("(k p) j -> p k j", p=128)
                                    ).then_inc(sl.h, 16)
                        sl.n += 32

                @blk.sync
                def _(sy):
                    for m in range(NCH + 2):
                        if m < NCH:
                            WG(sy, sTwo.h, two0 + m - 1)
                            sl = sOLs[m % 2]
                            for h in range(H):
                                sy.dma_start(
                                    out=ogl[:, (m % 2) * 2048 + h * 512:
                                            (m % 2) * 2048 + (h + 1) * 512],
                                    in_=G.og_sp[h * NCH + m, :, :]).then_inc(sl.h, 16)
                            sl.n += 64
                        if m >= 2:
                            ms = m - 2
                            WG(sy, sWoE.h, we0 + ms + 1)
                            sl = sSSs[ms % 2]
                            sy.dma_start(out=G.s_sp[ms * 128:(ms + 1) * 128, :],
                                         in_=s_st[:, (ms % 2) * D:(ms % 2 + 1) * D]
                                         ).then_inc(sl.h, 16)
                            sl.n += 16
                    WG(sy, sSSs[0].h, sSSs[0].n)
                    WG(sy, sSSs[1].h, sSSs[1].n)

                @blk.tensor
                def _(te):
                    te.wait_ge(sWO2.h, wl0 + 16 * 16)
                    for m in range(NCH):
                        WG(te, sOLs[m % 2].h, ol0[m % 2] + 64 * (m // 2 + 1))
                        WG(te, sWoE.h, we0 + m - 1)
                        ps = wopss[m % 2]
                        for k in range(16):
                            for n in range(2):
                                ins = te.matmul(
                                    ps[:, n * 512:(n + 1) * 512],
                                    ogl[:, (m % 2) * 2048 + k * 128:
                                        (m % 2) * 2048 + (k + 1) * 128],
                                    wo_sb[:, k * D + n * 512:k * D + (n + 1) * 512],
                                    start=(k == 0), stop=(k == 15))
                        sTwo.inc(ins)

                @blk.scalar
                def _(sc):
                    for m in range(NCH):
                        WG(sc, sTwo.h, two0 + m + 1)
                        WG(sc, sSSs[m % 2].h, ss0[m % 2] + 16 * (m // 2))
                        sc.copy(out=s_st[:, (m % 2) * D:(m % 2 + 1) * D],
                                in_=wopss[m % 2][:, :]).then_inc(sWoE.h, 1)
                        sWoE.n += 1
                we_end = sWoE.n

        inner2 = ExitStack()
        with inner2:
            SBi = lambda n, sh, dt: inner2.enter_context(nc.sbuf_tensor(f"{n}{s}", sh, dt))
            hT = SBi("hT", [128, HM * 1024], BF16)
            sg_sc = SBi("sg_sc", [128, 2 * 1024], BF16)
            g2_sc = SBi("g2_sc", [128, 2 * 1024], F32)
            xc_sb = SBi("xc_sb", [128, 2 * D], F32)
            sc_sb = SBi("sc_sb", [128, 2 * D], BF16)
            t_sb = SBi("t_sb", [128, 2 * D], F32)
            o_sb = SBi("o_sb", [128, 2 * D], F32)
            fgps, fups = P2, P3
            dpss = (P0, P1)
            with nc.Block() as blk:
                wl1 = sWD.n
                for half in range(2):
                    b = dict(fw=(fw_pre if half == 0 else [sFWs[0].n, sFWs[1].n]),
                             tg=sTg.n, tup=sTup.n,
                             fe=sFE.n, ht=sHT.n, td=sTd.n, fe2=sFE2.n, cv=sCv.n,
                             out=[sOUTs[0].n, sOUTs[1].n],
                             xl=[sXFs[0].n, sXFs[1].n])
                    tok0 = half * 1024

                    @blk.gpsimd
                    def _(g, half=half, b=b):
                        for m in range(2 if half == 0 else 0, HM):
                            WG(g, sTup.h, b["tup"] + m - 1)
                            par = m % 2
                            sl = sFWs[par]
                            g.dma_start(out=wgu[:, par * 1024:(par + 1) * 1024]
                                        .rearrange("p (k j) -> p k j", j=128),
                                        in_=G.fgw_d[m].rearrange("(k p) j -> p k j", p=128)
                                        ).then_inc(sl.h, 16)
                            g.dma_start(out=wgu[:, 2048 + par * 1024:
                                               2048 + (par + 1) * 1024]
                                        .rearrange("p (k j) -> p k j", j=128),
                                        in_=G.fuw_d[m].rearrange("(k p) j -> p k j", p=128)
                                        ).then_inc(sl.h, 16)
                            sl.n += 32

                    @blk.tensor
                    def _(te, half=half, b=b):
                        if half == 0:
                            WG(te, sWD.h, wl1)
                        for m in range(HM):
                            par = m % 2
                            WG(te, sFWs[par].h, b["fw"][par] + 32 * (m // 2 + 1))
                            WG(te, sFE.h, b["fe"] + m)
                            if half == 0 and m < 2:
                                pass
                            for n in range(2):
                                for k in range(KC):
                                    ins = te.matmul(
                                        fgps[:, n * 512:(n + 1) * 512],
                                        wgu[:, par * 1024 + k * 128:
                                            par * 1024 + (k + 1) * 128],
                                        G.xnT[:, k * T + tok0 + n * 512:
                                              k * T + tok0 + (n + 1) * 512],
                                        start=(k == 0), stop=(k == KC - 1))
                            sTg.inc(ins)
                            WG(te, sHT.h, b["ht"] + m)
                            for n in range(2):
                                for k in range(KC):
                                    ins = te.matmul(
                                        fups[:, n * 512:(n + 1) * 512],
                                        wgu[:, 2048 + par * 1024 + k * 128:
                                            2048 + par * 1024 + (k + 1) * 128],
                                        G.xnT[:, k * T + tok0 + n * 512:
                                              k * T + tok0 + (n + 1) * 512],
                                        start=(k == 0), stop=(k == KC - 1))
                            sTup.inc(ins)
                        WG(te, sHT.h, b["ht"] + HM)
                        for mc in range(8):
                            WG(te, sFE2.h, b["fe2"] + mc - 1)
                            if half == 0 and mc < 2:
                                WG(te, sWoE.h, we_end - 1 + mc)
                            ps = dpss[mc % 2]
                            for k in range(HM):
                                for n in range(2):
                                    ins = te.matmul(
                                        ps[:, n * 512:(n + 1) * 512],
                                        hT[:, k * 1024 + mc * 128:k * 1024 + (mc + 1) * 128],
                                        wd_sb[:, k * D + n * 512:k * D + (n + 1) * 512],
                                        start=(k == 0), stop=(k == HM - 1))
                            sTd.inc(ins)

                    @blk.scalar
                    def _(sc, half=half, b=b):
                        for m in range(HM):
                            WG(sc, sTg.h, b["tg"] + m + 1)
                            WG(sc, sHT.h, b["ht"] + m - 1)
                            sc.activation(out=sg_sc[:, (m % 2) * 1024:(m % 2 + 1) * 1024],
                                          in_=fgps[:, :], func=AF.Silu,
                                          scale=Q127).then_inc(sFE.h, 1)
                            sFE.n += 1
                        for mc in range(8):
                            WG(sc, sTd.h, b["td"] + mc + 1)
                            WG(sc, sCv.h, b["cv"] + mc - 1)
                            sc.activation(out=g2_sc[:, (mc % 2) * D:(mc % 2 + 1) * D],
                                          in_=dpss[mc % 2][:, :], func=AF.Silu,
                                          scale=Q127 * Q127).then_inc(sFE2.h, 1)
                            sFE2.n += 1

                    @blk.vector
                    def _(v, half=half, b=b):
                        for m in range(HM):
                            WG(v, sTup.h, b["tup"] + m + 1)
                            WG(v, sFE.h, b["fe"] + m + 1)
                            ins = v.tensor_mul(out=hT[:, m * 1024:(m + 1) * 1024],
                                               in0=sg_sc[:, (m % 2) * 1024:(m % 2 + 1) * 1024],
                                               in1=fups[:, :])
                            sHT.inc(ins)
                        for mc in range(8):
                            WG(v, sFE2.h, b["fe2"] + mc + 1)
                            WG(v, sXFs[mc % 2].h, b["xl"][mc % 2] + 32 * (mc // 2 + 1))
                            WG(v, sOUTs[mc % 2].h, b["out"][mc % 2] + 16 * (mc // 2))
                            v.tensor_mul(out=t_sb[:, (mc % 2) * D:(mc % 2 + 1) * D],
                                         in0=sc_sb[:, (mc % 2) * D:(mc % 2 + 1) * D],
                                         in1=g2_sc[:, (mc % 2) * D:(mc % 2 + 1) * D])
                            v.drain()
                            ins = v.tensor_add(out=o_sb[:, (mc % 2) * D:(mc % 2 + 1) * D],
                                               in0=t_sb[:, (mc % 2) * D:(mc % 2 + 1) * D],
                                               in1=xc_sb[:, (mc % 2) * D:(mc % 2 + 1) * D])
                            sCv.inc(ins)

                    @blk.sync
                    def _(sy, half=half, b=b):
                        for it in range(10):
                            if it < 8:
                                mc = it
                                row = s * T + tok0 + mc * 128
                                WG(sy, sCv.h, b["cv"] + mc - 1)
                                sl = sXFs[mc % 2]
                                sy.dma_start(out=xc_sb[:, (mc % 2) * D:(mc % 2 + 1) * D],
                                             in_=G.x_d[row:row + 128, :]).then_inc(sl.h, 16)
                                sy.dma_start(out=sc_sb[:, (mc % 2) * D:(mc % 2 + 1) * D],
                                             in_=G.s_sp[tok0 + mc * 128:
                                                        tok0 + mc * 128 + 128, :]
                                             ).then_inc(sl.h, 16)
                                sl.n += 32
                            if it >= 2:
                                mc = it - 2
                                row = s * T + tok0 + mc * 128
                                WG(sy, sCv.h, b["cv"] + mc + 1)
                                sl = sOUTs[mc % 2]
                                sy.dma_start(out=G.out_d[row:row + 128, :],
                                             in_=o_sb[:, (mc % 2) * D:(mc % 2 + 1) * D]
                                             ).then_inc(sl.h, 16)
                                sl.n += 16


_NC_CACHE = None


def kernel(L0_attn_x, L0_attn_Wq, L0_attn_Wk, L0_attn_Wv, L0_attn_Wg,
           L0_attn_Wo, L0_attn_gn_weight, L0_ffn_w_gate, L0_ffn_w_up,
           L0_ffn_w_down):
    global _NC_CACHE
    x = np.ascontiguousarray(np.asarray(L0_attn_x, dtype=np.float32))
    wq = np.ascontiguousarray(np.asarray(L0_attn_Wq, dtype=np.float32))
    wk = np.ascontiguousarray(np.asarray(L0_attn_Wk, dtype=np.float32))
    wv = np.ascontiguousarray(np.asarray(L0_attn_Wv, dtype=np.float32))
    wg = np.ascontiguousarray(np.asarray(L0_attn_Wg, dtype=np.float32))
    wo = np.ascontiguousarray(np.asarray(L0_attn_Wo, dtype=np.float32))
    gn = np.asarray(L0_attn_gn_weight, dtype=np.float32)
    gww = np.ascontiguousarray(np.broadcast_to(np.tile(gn, H)[None, :], (128, H * DV)))
    fg = np.asarray(L0_ffn_w_gate, dtype=np.int32)
    fu = np.asarray(L0_ffn_w_up, dtype=np.int32)
    fd = np.asarray(L0_ffn_w_down, dtype=np.int32)
    fgw = np.ascontiguousarray(fg.T.reshape(D, HM, 128).transpose(1, 0, 2))
    fuw = np.ascontiguousarray(fu.T.reshape(D, HM, 128).transpose(1, 0, 2))
    fdw = np.ascontiguousarray(fd.T)

    cosb, sinb, cqb, dpb, ckv, idn = _host_consts()

    if _NC_CACHE is None:
        _NC_CACHE = build_nc()
    nc = _NC_CACHE

    in_maps = []
    for c in range(NCORE):
        xc = np.ascontiguousarray(x[c * SEQ:(c + 1) * SEQ].reshape(SEQ * T, D))
        in_maps.append({
            "x": xc, "wq": wq, "wk": wk, "wv": wv, "wgt": wg, "wo": wo,
            "gww": gww, "fgw": fgw, "fuw": fuw, "fdw": fdw,
            "cosb": cosb, "sinb": sinb, "cqb": cqb, "dpb": dpb, "ckv": ckv,
            "idn": idn,
        })
    res = run_bass_kernel_spmd(nc, in_maps, core_ids=list(range(NCORE)))
    outs = [r["out"].reshape(SEQ, T, D) for r in res.results]
    return np.concatenate(outs, axis=0)


# version 14
# speedup vs baseline: 1.0340x; 1.0340x over previous
import numpy as np
import ml_dtypes
from contextlib import ExitStack
import concourse.bass as bass
import concourse.mybir as mybir
from concourse.bass_utils import run_bass_kernel_spmd

BF = ml_dtypes.bfloat16
F32 = mybir.dt.float32
BF16 = mybir.dt.bfloat16
I32 = mybir.dt.int32
AF = mybir.ActivationFunctionType
ALU = mybir.AluOpType
AX = mybir.AxisListType.X

NCORE = 8
SEQ = 2
T = 2048
D = 1024
H = 4
DV = 512
CH = 128
NCH = T // CH
KC = D // 128
HID = 3072
HM = HID // 128
EPS = 1e-5
MAGIC = 0x5F3759DF
Q127 = 1.0 / 127.0
LAG = 5

GAMMAS = [1.0 - 2.0 ** (-5.0 - h) for h in range(H)]

PHASES = ("ln", "attn", "wo", "ffn")
DEBUG = False
NHEADS = H
LN_STAGE = 7


def WG(eng, sem_h, val):
    if val > 0:
        eng.wait_ge(sem_h, val)


class Sem:
    def __init__(self, nc, name):
        self.h = nc.semaphore(name).__enter__()
        self.n = 0

    def inc(self, ins, k=1):
        ins.then_inc(self.h, k)
        self.n += k
        return self.n


def _host_consts():
    f = np.arange(128, dtype=np.float64)
    inv_freq = 1.0 / (10000.0 ** (2.0 * f / 256.0))
    t = np.arange(T, dtype=np.float64)
    ang = np.outer(inv_freq, t)
    cosb = np.cos(ang).astype(BF)
    sinb = np.sin(ang).astype(BF)
    tt = np.arange(512) % 128
    cq = np.zeros((128, H * 512), dtype=np.float64)
    for h in range(H):
        cq[:, h * 512:(h + 1) * 512] = (GAMMAS[h] ** (tt + 1.0)) / 16.0
    cqb = cq.astype(BF)
    j = np.arange(128)[:, None]
    i = np.arange(128)[None, :]
    dp = np.zeros((128, H * 128), dtype=np.float64)
    for h in range(H):
        dp[:, h * 128:(h + 1) * 128] = np.where(i >= j, GAMMAS[h] ** (-(j + 1.0)), 0.0)
    dpb = dp.astype(BF)
    ck = np.zeros((128, H), dtype=np.float64)
    for h in range(H):
        ck[:, h] = GAMMAS[h] ** (127.0 - np.arange(128))
    ckv = ck.astype(np.float32)
    idn = np.eye(128, dtype=BF)
    return cosb, sinb, cqb, dpb, ckv, idn


def _quake_rsqrt(v, out_ap, in_ap, qi, qf, sem=None):
    n = in_ap.shape[-1]
    v.tensor_single_scalar(out=qi[:, 0:n], in_=in_ap.bitcast(I32),
                           scalar=1, op=ALU.arith_shift_right)
    v.drain()
    v.tensor_scalar(out=qi[:, 0:n], in0=qi[:, 0:n], scalar1=MAGIC, scalar2=-1,
                    op0=ALU.subtract, op1=ALU.mult)
    v.drain()
    y = qi.ap().bitcast(F32)[:, 0:n]
    v.tensor_mul(out=qf[:, 0:n], in0=y, in1=y)
    v.drain()
    v.tensor_mul(out=qf[:, 0:n], in0=qf[:, 0:n], in1=in_ap)
    v.drain()
    v.tensor_scalar(out=qf[:, 0:n], in0=qf[:, 0:n], scalar1=-0.5, scalar2=1.5,
                    op0=ALU.mult, op1=ALU.add)
    v.drain()
    v.tensor_mul(out=out_ap, in0=y, in1=qf[:, 0:n])
    v.drain()
    v.tensor_mul(out=qf[:, 0:n], in0=out_ap, in1=out_ap)
    v.drain()
    v.tensor_mul(out=qf[:, 0:n], in0=qf[:, 0:n], in1=in_ap)
    v.drain()
    v.tensor_scalar(out=qf[:, 0:n], in0=qf[:, 0:n], scalar1=-0.5, scalar2=1.5,
                    op0=ALU.mult, op1=ALU.add)
    v.drain()
    ins = v.tensor_mul(out=out_ap, in0=out_ap, in1=qf[:, 0:n])
    if sem is not None:
        sem.inc(ins)
    v.drain()


def build_nc():
    nc = bass.Bass("TRN2", target_bir_lowering=False)

    x_d = nc.dram_tensor("x", [SEQ * T, D], F32, kind="ExternalInput")
    wq_d = nc.dram_tensor("wq", [D, D], F32, kind="ExternalInput")
    wk_d = nc.dram_tensor("wk", [D, D], F32, kind="ExternalInput")
    wv_d = nc.dram_tensor("wv", [D, 2048], F32, kind="ExternalInput")
    wgt_d = nc.dram_tensor("wgt", [D, 2048], F32, kind="ExternalInput")
    wo_d = nc.dram_tensor("wo", [2048, D], F32, kind="ExternalInput")
    gww_d = nc.dram_tensor("gww", [128, 2048], F32, kind="ExternalInput")
    fgw_d = nc.dram_tensor("fgw", [HM, D, 128], I32, kind="ExternalInput")
    fuw_d = nc.dram_tensor("fuw", [HM, D, 128], I32, kind="ExternalInput")
    fdw_d = nc.dram_tensor("fdw", [HID, D], I32, kind="ExternalInput")
    cos_d = nc.dram_tensor("cosb", [128, T], BF16, kind="ExternalInput")
    sin_d = nc.dram_tensor("sinb", [128, T], BF16, kind="ExternalInput")
    cq_d = nc.dram_tensor("cqb", [128, H * 512], BF16, kind="ExternalInput")
    dp_d = nc.dram_tensor("dpb", [128, H * 128], BF16, kind="ExternalInput")
    ck_d = nc.dram_tensor("ckv", [128, H], F32, kind="ExternalInput")
    id_d = nc.dram_tensor("idn", [128, 128], BF16, kind="ExternalInput")

    spill_kind = "ExternalOutput" if DEBUG else "Internal"
    og_sp = nc.dram_tensor("og_spill", [H * NCH, 128, 512], BF16, kind=spill_kind)
    s_sp = nc.dram_tensor("s_spill", [T, D], BF16, kind=spill_kind)
    out_d = nc.dram_tensor("out", [SEQ * T, D], F32, kind="ExternalOutput")
    dbg_xnT = (nc.dram_tensor("dbg_xnT", [128, KC * T], BF16, kind="ExternalOutput")
               if DEBUG else None)

    A = nc.alloc_sbuf_tensor
    cos_sb = A("cos_sb", [128, T], BF16)
    sin_sb = A("sin_sb", [128, T], BF16)
    cq_sb = A("cq_sb", [128, H * 512], BF16)
    dp_sb = A("dp_sb", [128, H * 128], BF16)
    ck_sb = A("ck_sb", [128, H], F32)
    id_sb = A("id_sb", [128, 128], BF16)
    gw_sb = A("gw_sb", [128, 2048], BF16)
    xnT = A("xnT", [128, KC * T], BF16)

    S0 = A("S0", [128, 1024], BF16)
    S1 = A("S1", [128, 1024], BF16)
    Sb = [S0, S1]
    ss_sb = A("ss_sb", [128, 8], F32)
    rr_sb = A("rr_sb", [128, 8], F32)
    qk1 = A("qk1", [128, 8], I32)
    qk2 = A("qk2", [128, 8], F32)
    qk3 = A("qk3", [128, 8], F32)
    lm_sb = A("lm_sb", [128, 8], F32)
    lr_sb = A("lr_sb", [128, 4], F32)
    lb_sb = A("lb_sb", [128, 4], F32)

    sems = {n: Sem(nc, n) for n in
            ("sWL sWQK sWO2 sWD sWH sLNq sLNv sLNs sTln sLNe "
             "sTp sPE sRP sGT sTa sTk sTo sTu sTog "
             "sAT sS sRr sOT sOGv sKT sOE sOGE "
             "sOL0 sOL1 sTwo sWoE sTd sTg sTup sFE sHT sFE2 sCv sCL "
             "sXL0 sXL1 sXL2 sXL3 sSP0 sSP1 sSS0 sSS1 "
             "sFW0 sFW1 sXF0 sXF1 sOUT0 sOUT1 "
             "sWG0 sWG1 sWG2 sWG3").split()}
    g = dict(sems)
    g.update(dict(nc=nc, x_d=x_d, wq_d=wq_d, wk_d=wk_d, wv_d=wv_d, wgt_d=wgt_d,
                  wo_d=wo_d, fgw_d=fgw_d, fuw_d=fuw_d, fdw_d=fdw_d,
                  og_sp=og_sp, s_sp=s_sp, out_d=out_d,
                  cos_sb=cos_sb, sin_sb=sin_sb, cq_sb=cq_sb, dp_sb=dp_sb,
                  ck_sb=ck_sb, id_sb=id_sb, gw_sb=gw_sb, xnT=xnT, Sb=Sb,
                  ss_sb=ss_sb, rr_sb=rr_sb, qk1=qk1, qk2=qk2, qk3=qk3,
                  lm_sb=lm_sb, lr_sb=lr_sb, lb_sb=lb_sb))
    G = type("G", (), g)

    with nc.sbuf_tensor("gwf_sb", [128, 2048], F32) as gwf_sb, nc.Block() as blk:
        @blk.sync
        def _(sy):
            for dst, src in ((cos_sb, cos_d), (sin_sb, sin_d), (cq_sb, cq_d),
                             (dp_sb, dp_d), (ck_sb, ck_d), (id_sb, id_d),
                             (gwf_sb, gww_d)):
                sy.dma_start(out=dst[:, :], in_=src[:, :]).then_inc(G.sCL.h, 16)
            G.sCL.n += 7 * 16

        @blk.vector
        def _(v):
            WG(v, G.sCL.h, G.sCL.n)
            v.tensor_copy(out=gw_sb[:, :], in_=gwf_sb[:, :])

    nseq = 1 if DEBUG else SEQ
    for s in range(nseq):
        _phase_attn(G, s)
        _phase_wo_ffn(G, s)
    if DEBUG:
        with nc.Block() as blk:
            @blk.sync
            def _(sy):
                sy.dma_start(out=dbg_xnT[:, :], in_=xnT[:, :]).then_inc(G.sCL.h, 16)
                G.sCL.n += 16
                WG(sy, G.sCL.h, G.sCL.n)

    with nc.Block() as blk:
        @blk.sync
        def _(sy):
            WG(sy, G.sOUT0.h, G.sOUT0.n)
            WG(sy, G.sOUT1.h, G.sOUT1.n)

    return nc


def _emit_qk_evac(G, sc, b, eraw, U_ps, pair):
    WG(sc, G.sRP.h, b["rp"] + pair - 1)
    for d in range(2):
        grp = 2 * pair + d
        WG(sc, G.sTp.h, b["tp"] + grp + 1)
        par = grp % 2
        gp = pair % 2
        sc.copy(out=eraw[:, gp * 1024 + d * 512:gp * 1024 + (d + 1) * 512],
                in_=U_ps[:, par * 512:(par + 1) * 512]).then_inc(G.sPE.h, 1)
        G.sPE.n += 1


def _emit_ln(G, s, blk, x_sb, scr, xn_sb, lnps, extra_cb=None):
    nc = G.nc
    sLNq, sLNv, sLNs, sTln, sLNe = G.sLNq, G.sLNv, G.sLNs, G.sTln, G.sLNe
    sXLs = [G.sXL0, G.sXL1, G.sXL2, G.sXL3]
    x0s = [t.n for t in sXLs]
    q0, v0, s0, t0, e0 = sLNq.n, sLNv.n, sLNs.n, sTln.n, sLNe.n

    @blk.sync
    def _(sy):
        for tc in range(NCH):
            WG(sy, sLNs.h, s0 + tc - 3)
            WG(sy, sLNq.h, q0 + tc - 3)
            sl = sXLs[tc % 4]
            sy.dma_start(out=x_sb[:, (tc % 4) * D:(tc % 4 + 1) * D],
                         in_=G.x_d[s * T + tc * 128: s * T + (tc + 1) * 128, :]
                         ).then_inc(sl.h, 16)
            sl.n += 16

    @blk.vector
    def _(v):
        for w in range(4):
            for jj in range(4):
                tc = 4 * w + jj
                WG(v, sXLs[tc % 4].h, x0s[tc % 4] + 16 * (tc // 4 + 1))
                v.reduce_sum(out=G.lm_sb[:, jj:jj + 1],
                             in_=x_sb[:, (tc % 4) * D:(tc % 4 + 1) * D], axis=AX)
            WG(v, sLNq.h, q0 + 4 * (w + 1))
            v.drain()
            m1 = G.lm_sb[:, 0:4]
            m2 = G.lm_sb[:, 4:8]
            v.tensor_scalar_mul(out=G.qk2[:, 0:4], in0=m1, scalar1=1.0 / D)
            v.tensor_scalar(out=G.qk2[:, 4:8], in0=m2, scalar1=1.0 / D, scalar2=EPS,
                            op0=ALU.mult, op1=ALU.add)
            v.drain()
            v.tensor_mul(out=G.qk3[:, 0:4], in0=G.qk2[:, 0:4], in1=G.qk2[:, 0:4])
            v.drain()
            v.tensor_sub(out=G.qk3[:, 4:8], in0=G.qk2[:, 4:8], in1=G.qk3[:, 0:4])
            v.drain()
            _quake_rsqrt(v, G.lr_sb[:, 0:4], G.qk3[:, 4:8], G.qk1, G.qk2)
            ins = v.scalar_tensor_tensor(out=G.lb_sb[:, 0:4], in0=G.lm_sb[:, 0:4],
                                         scalar=-1.0 / D, in1=G.lr_sb[:, 0:4],
                                         op0=ALU.mult, op1=ALU.mult)
            sLNv.inc(ins)
            v.drain()
            for jj in range(4):
                tc = 4 * w + jj
                WG(v, sTln.h, t0 + tc - 3)
                ins = v.tensor_scalar(out=xn_sb[:, (tc % 4) * D:(tc % 4 + 1) * D],
                                      in0=x_sb[:, (tc % 4) * D:(tc % 4 + 1) * D],
                                      scalar1=G.lr_sb[:, jj:jj + 1],
                                      scalar2=G.lb_sb[:, jj:jj + 1],
                                      op0=ALU.mult, op1=ALU.add)
                sLNs.inc(ins)

    @blk.scalar
    def _(sc):
        def evac(tc):
            WG(sc, sTln.h, t0 + tc + 1)
            for k in range(KC):
                ins = sc.copy(
                    out=G.xnT[:, k * T + tc * 128: k * T + (tc + 1) * 128],
                    in_=lnps[:, k * 128:(k + 1) * 128])
            ins.then_inc(sLNe.h, 1)
            sLNe.n += 1

        for w in range(4):
            for jj in range(4):
                tc = 4 * w + jj
                WG(sc, sXLs[tc % 4].h, x0s[tc % 4] + 16 * (tc // 4 + 1))
                WG(sc, sLNv.h, v0 + w)
                sc.activation(out=scr[:, :],
                              in_=x_sb[:, (tc % 4) * D:(tc % 4 + 1) * D],
                              func=AF.Square,
                              accum_out=G.lm_sb[:, 4 + jj:5 + jj]
                              ).then_inc(sLNq.h, 1)
                sLNq.n += 1
                sc.drain()
            if w >= 1:
                for jj in range(4):
                    evac(4 * (w - 1) + jj)
                if extra_cb is not None:
                    extra_cb(sc, w)
        for tc in range(12, NCH):
            evac(tc)

    def ln_tr(te, tc):
        WG(te, sLNs.h, s0 + tc + 1)
        WG(te, sLNe.h, e0 + tc)
        for k in range(KC):
            ins = te.transpose(
                lnps[:, k * 128:(k + 1) * 128],
                xn_sb[:, (tc % 4) * D + k * 128:(tc % 4) * D + (k + 1) * 128],
                G.id_sb[:, :])
        ins.then_inc(G.sTln.h, 1)
        G.sTln.n += 1

    return e0, ln_tr


def _phase_attn(G, s):
    nc = G.nc
    (sWL, sWH, sTp, sPE, sRP, sGT, sTa, sTk, sTo, sTu, sTog, sAT, sS, sRr,
     sOT, sOGv, sKT, sOE, sOGE) = (G.sWL, G.sWH, G.sTp, G.sPE, G.sRP, G.sGT,
                                   G.sTa, G.sTk, G.sTo, G.sTu, G.sTog, G.sAT,
                                   G.sS, G.sRr, G.sOT, G.sOGv, G.sKT, G.sOE,
                                   G.sOGE)
    sSPs = [G.sSP0, G.sSP1]
    sLNe = G.sLNe
    sWQK = G.sWQK
    Sb = G.Sb
    es = ExitStack()
    SB = lambda n, sh, dt: es.enter_context(nc.sbuf_tensor(f"{n}{s}", sh, dt))
    PS = lambda n, sh, dt: es.enter_context(nc.psum_tensor(f"{n}{s}", sh, dt))
    wq_sb = SB("wq_sb", [128, KC * D], BF16)
    wk_sb = SB("wk_sb", [128, KC * D], BF16)
    wv_sb = SB("wv_sb", [128, KC * 512], BF16)
    wg_sb = SB("wg_sb", [128, KC * 512], BF16)
    qT = SB("qT", [128, 2 * T], BF16)
    kT = SB("kT", [128, 2 * T], BF16)
    vtb = SB("vtb", [128, NCH * 512], BF16)
    ggb = SB("ggb", [128, NCH * 512], BF16)
    ropem = SB("rope_m", [128, 6 * 512], BF16)
    eraw = SB("eraw", [128, 4 * 512], BF16)
    gtmp = SB("gtmp", [128, 2 * 512], BF16)
    atm = SB("atm", [128, 2 * 128], BF16)
    ktok = SB("ktok", [128, 2 * 256], BF16)
    o_win = SB("o_win", [128, 6 * 512], BF16)
    ot8 = SB("ot8", [128, 4 * 512], BF16)
    og_sb = SB("og_sb", [128, 4 * 512], BF16)
    og_st = SB("og_st", [128, 2 * 512], BF16)
    x_sb = SB("ln_x", [128, 4 * D], F32)
    scr = SB("ln_scr", [128, D], F32)
    xn_sb = SB("ln_xn", [128, 4 * D], BF16)
    U_ps = PS("U_ps", [128, 1024], F32)
    o_ps = PS("o_ps", [128, 2 * 512], F32)
    at_ps = PS("at_ps", [128, 2 * 128], F32)
    kt_ps = PS("kt_ps", [128, 2 * 256], BF16)
    ogT_ps = PS("ogT_ps", [128, 512], BF16)
    lnps = PS("lnps", [128, D], BF16)
    with es, nc.Block() as blk:
        wl0 = sWQK.n
        b_h0 = dict(tp=sTp.n, rp=sRP.n, pe=sPE.n)

        def _ln_extra(sc, w):
            _emit_qk_evac(G, sc, b_h0, eraw, U_ps, w - 1)

        ln_e0, ln_tr = _emit_ln(G, s, blk, x_sb, scr, xn_sb, lnps, _ln_extra)

        @blk.gpsimd
        def _(g):
            for k in range(KC):
                g.dma_start(out=wq_sb[:, k * D:(k + 1) * D],
                            in_=G.wq_d[k * 128:(k + 1) * 128, :]).then_inc(sWQK.h, 16)
                g.dma_start(out=wk_sb[:, k * D:(k + 1) * D],
                            in_=G.wk_d[k * 128:(k + 1) * 128, :]).then_inc(sWQK.h, 16)
            sWQK.n += 16 * 2 * KC

        def emit_og_gp(g, bo):
            for j in range(NCH):
                WG(g, sOT.h, bo["ot"] + j + 1)
                WG(g, sGT.h, bo["gt"] + j + 1)
                WG(g, sTog.h, bo["tog"] + j - 3)
                ins = g.tensor_mul(
                    out=og_sb[:, (j % 4) * 512:(j % 4 + 1) * 512],
                    in0=ot8[:, (j % 4) * 512:(j % 4 + 1) * 512],
                    in1=ggb[:, j * 512:(j + 1) * 512])
                sOGv.inc(ins)

        bprev = None
        for h in range(NHEADS):
            tp_gate = sTp.n

            @blk.gpsimd
            def _(g, h=h, tp_gate=tp_gate, bprev=bprev):
                WG(g, sTp.h, tp_gate)
                for k in range(KC):
                    g.dma_start(out=wv_sb[:, k * 512:(k + 1) * 512],
                                in_=G.wv_d[k * 128:(k + 1) * 128, h * 512:(h + 1) * 512]
                                ).then_inc(sWH.h, 16)
                    g.dma_start(out=wg_sb[:, k * 512:(k + 1) * 512],
                                in_=G.wgt_d[k * 128:(k + 1) * 128, h * 512:(h + 1) * 512]
                                ).then_inc(sWH.h, 16)
                sWH.n += 16 * 2 * KC
                if bprev is not None:
                    emit_og_gp(g, bprev)
            wl_h = sWH.n

            b = dict(tp=sTp.n, pe=(b_h0["pe"] if h == 0 else sPE.n), rp=sRP.n,
                     gt=sGT.n,
                     ta=sTa.n, tk=sTk.n, to=sTo.n, tu=sTu.n, tog=sTog.n,
                     at=sAT.n, st=sS.n, rr=sRr.n, ot=sOT.n, ogv=sOGv.n,
                     kt=sKT.n, oe=sOE.n, oge=sOGE.n,
                     sp=[sSPs[0].n, sSPs[1].n])

            @blk.tensor
            def _(te, h=h, b=b, wl_h=wl_h):
                if h == 0:
                    WG(te, sWQK.h, wl0 + 16 * 2 * KC)
                WG(te, sS.h, b["st"])
                grp = 0
                for wsb in (wq_sb, wk_sb):
                    for n in range(4):
                        if h == 0 and wsb is wq_sb:
                            for tc in range(4 * n, 4 * n + 4):
                                ln_tr(te, tc)
                        if h == 0:
                            WG(te, sLNe.h, ln_e0 + 4 * (n + 1))
                        for d in range(2):
                            par = grp % 2
                            WG(te, sPE.h, b["pe"] + grp - 1)
                            for k in range(KC):
                                ins = te.matmul(
                                    U_ps[:, par * 512:(par + 1) * 512],
                                    wsb[:, k * D + (2 * h + d) * 128:
                                        k * D + (2 * h + d + 1) * 128],
                                    G.xnT[:, k * T + n * 512:k * T + (n + 1) * 512],
                                    start=(k == 0), stop=(k == KC - 1))
                            sTp.inc(ins)
                            grp += 1
                WG(te, sWH.h, wl_h)
                for wsb in (wv_sb, wg_sb):
                    for c in range(NCH):
                        par = grp % 2
                        WG(te, sPE.h, b["pe"] + grp - 1)
                        for k in range(KC):
                            ins = te.matmul(
                                U_ps[:, par * 512:(par + 1) * 512],
                                G.xnT[:, k * T + c * 128:k * T + (c + 1) * 128],
                                wsb[:, k * 512:(k + 1) * 512],
                                start=(k == 0), stop=(k == KC - 1))
                        sTp.inc(ins)
                        grp += 1

            @blk.scalar
            def _(sc, h=h, b=b):
                WG(sc, sTo.h, b["to"])
                WG(sc, sTu.h, b["tu"])
                first_pair = 3 if h == 0 else 0
                for pair in range(first_pair, 8):
                    _emit_qk_evac(G, sc, b, eraw, U_ps, pair)
                grp = 16
                for c in range(NCH):
                    WG(sc, sTp.h, b["tp"] + grp + 1)
                    par = grp % 2
                    sc.copy(out=vtb[:, c * 512:(c + 1) * 512],
                            in_=U_ps[:, par * 512:(par + 1) * 512]).then_inc(sPE.h, 1)
                    sPE.n += 1
                    grp += 1
                for c in range(NCH):
                    WG(sc, sTp.h, b["tp"] + grp + 1)
                    WG(sc, sGT.h, b["gt"] + c - 1)
                    par = grp % 2
                    sc.activation(out=gtmp[:, (c % 2) * 512:(c % 2 + 1) * 512],
                                  in_=U_ps[:, par * 512:(par + 1) * 512],
                                  func=AF.Silu).then_inc(sPE.h, 1)
                    sPE.n += 1
                    grp += 1

            @blk.vector
            def _(v, h=h, b=b):
                WG(v, sTo.h, b["to"])
                WG(v, sTa.h, b["ta"])
                WG(v, sTk.h, b["tk"])
                v.memset(Sb[0][:, :], 0.0)
                for mi, (dst, has_cq) in enumerate(((qT, True), (kT, False))):
                    for n in range(4):
                        pair = mi * 4 + n
                        gp = pair % 2
                        WG(v, sPE.h, b["pe"] + 2 * pair + 2)
                        v.drain()
                        e0r = eraw[:, gp * 1024:gp * 1024 + 512]
                        e1r = eraw[:, gp * 1024 + 512:gp * 1024 + 1024]
                        e0 = dst[:, 0 * T + n * 512:0 * T + (n + 1) * 512]
                        e1 = dst[:, 1 * T + n * 512:1 * T + (n + 1) * 512]
                        c_sl = G.cos_sb[:, n * 512:(n + 1) * 512]
                        s_sl = G.sin_sb[:, n * 512:(n + 1) * 512]
                        p0 = ropem[:, 0:512]
                        p1 = ropem[:, 512:1024]
                        p2 = ropem[:, 1024:1536]
                        p3 = ropem[:, 1536:2048]
                        if has_cq:
                            cqs = G.cq_sb[:, h * 512:(h + 1) * 512]
                            a0 = ropem[:, 2048:2560]
                            a1 = ropem[:, 2560:3072]
                            v.tensor_mul(out=a0, in0=e0r, in1=cqs)
                            v.tensor_mul(out=a1, in0=e1r, in1=cqs)
                            v.drain()
                            e0r, e1r = a0, a1
                        v.tensor_mul(out=p0, in0=e0r, in1=c_sl)
                        v.tensor_mul(out=p1, in0=e1r, in1=s_sl)
                        v.tensor_mul(out=p2, in0=e0r, in1=s_sl)
                        v.tensor_mul(out=p3, in0=e1r, in1=c_sl)
                        v.drain()
                        v.tensor_sub(out=e0, in0=p0, in1=p1)
                        ins = v.tensor_add(out=e1, in0=p3, in1=p2)
                        sRP.inc(ins)
                for c in range(NCH):
                    WG(v, sPE.h, b["pe"] + 32 + c + 1)
                    WG(v, sOGv.h, b["ogv"] + c - 15)
                    ins = v.tensor_mul(out=ggb[:, c * 512:(c + 1) * 512],
                                       in0=gtmp[:, (c % 2) * 512:(c % 2 + 1) * 512],
                                       in1=G.gw_sb[:, h * 512:(h + 1) * 512])
                    sGT.inc(ins)

            @blk.tensor
            def _(te, h=h, b=b):
                WG(te, sRP.h, b["rp"] + 8)
                WG(te, sPE.h, b["pe"] + 32 + NCH)

                def og_tr(c):
                    WG(te, sOGv.h, b["ogv"] + c + 1)
                    WG(te, sOGE.h, b["oge"] + c)
                    for jj in range(4):
                        ins = te.transpose(
                            ogT_ps[:, jj * 128:(jj + 1) * 128],
                            og_sb[:, (c % 4) * 512 + jj * 128:
                                  (c % 4) * 512 + (jj + 1) * 128],
                            G.id_sb[:, :])
                    sTog.inc(ins)

                def at_kt(cc):
                    par2 = cc % 2
                    WG(te, sAT.h, b["at"] + cc)
                    for d in range(2):
                        ins = te.matmul(at_ps[:, par2 * 128:(par2 + 1) * 128],
                                        kT[:, d * T + cc * 128:d * T + (cc + 1) * 128],
                                        qT[:, d * T + cc * 128:d * T + (cc + 1) * 128],
                                        start=(d == 0), stop=(d == 1))
                    sTa.inc(ins)
                    WG(te, sKT.h, b["kt"] + cc)
                    for d in range(2):
                        ins = te.transpose(
                            kt_ps[:, par2 * 256 + d * 128:par2 * 256 + (d + 1) * 128],
                            kT[:, d * T + cc * 128:d * T + (cc + 1) * 128],
                            G.id_sb[:, :])
                    sTk.inc(ins)

                at_kt(0)
                at_kt(1)
                for c in range(NCH):
                    par = c % 2
                    if c >= LAG:
                        og_tr(c - LAG)
                    WG(te, sAT.h, b["at"] + c + 1)
                    WG(te, sOE.h, b["oe"] + 2 * c - 2)
                    WG(te, sS.h, b["st"] + 2 * c)
                    ins = te.matmul(o_ps[:, par * 512:(par + 1) * 512],
                                    atm[:, par * 128:(par + 1) * 128],
                                    vtb[:, c * 512:(c + 1) * 512],
                                    start=True, stop=(c == 0))
                    if c > 0:
                        for d in range(2):
                            ins = te.matmul(o_ps[:, par * 512:(par + 1) * 512],
                                            qT[:, d * T + c * 128:d * T + (c + 1) * 128],
                                            Sb[c % 2][:, d * 512:(d + 1) * 512],
                                            start=False, stop=(d == 1))
                    sTo.inc(ins)
                    WG(te, sKT.h, b["kt"] + c + 1)
                    for d in range(2):
                        WG(te, sS.h, b["st"] + 2 * c + d - 1)
                        ins = te.matmul(U_ps[:, d * 512:(d + 1) * 512],
                                        ktok[:, par * 256 + d * 128:
                                             par * 256 + (d + 1) * 128],
                                        vtb[:, c * 512:(c + 1) * 512],
                                        start=True, stop=True)
                        sTu.inc(ins)
                    if c + 2 < NCH:
                        at_kt(c + 2)
                for c in range(NCH - LAG, NCH):
                    og_tr(c)

            @blk.vector
            def _(v, h=h, b=b):
                def mask(cc):
                    par2 = cc % 2
                    WG(v, sTa.h, b["ta"] + cc + 1)
                    WG(v, sTo.h, b["to"] + cc - 1)
                    ins = v.tensor_mul(out=atm[:, par2 * 128:(par2 + 1) * 128],
                                       in0=at_ps[:, par2 * 128:(par2 + 1) * 128],
                                       in1=G.dp_sb[:, h * 128:(h + 1) * 128])
                    sAT.inc(ins)

                mask(0)
                mask(1)
                for c in range(NCH):
                    w = c // 4
                    if c < NCH - 1:
                        for d in range(2):
                            WG(v, sTu.h, b["tu"] + 2 * c + d + 1)
                            ins = v.scalar_tensor_tensor(
                                out=Sb[(c + 1) % 2][:, d * 512:(d + 1) * 512],
                                in0=Sb[c % 2][:, d * 512:(d + 1) * 512],
                                scalar=float(GAMMAS[h] ** CH),
                                in1=U_ps[:, d * 512:(d + 1) * 512],
                                op0=ALU.mult, op1=ALU.add)
                            sS.inc(ins)
                    if c + 2 < NCH:
                        mask(c + 2)
                    if c % 4 == 3:
                        WG(v, sOE.h, b["oe"] + 8 * w + 8)
                        WG(v, sOT.h, b["ot"] + 4 * w - 4)
                        wp = w % 2
                        v.tensor_scalar(out=G.qk3[:, 0:4],
                                        in0=G.ss_sb[:, wp * 4:wp * 4 + 4],
                                        scalar1=1.0 / 512.0, scalar2=EPS,
                                        op0=ALU.mult, op1=ALU.add)
                        v.drain()
                        _quake_rsqrt(v, G.rr_sb[:, wp * 4:wp * 4 + 4],
                                     G.qk3[:, 0:4], G.qk1, G.qk2, sem=sRr)

            @blk.scalar
            def _(sc, h=h, b=b):
                def og_evac(c):
                    WG(sc, sTog.h, b["tog"] + c + 1)
                    WG(sc, sSPs[c % 2].h, b["sp"][c % 2] + 16 * (c // 2))
                    sc.copy(out=og_st[:, (c % 2) * 512:(c % 2 + 1) * 512],
                            in_=ogT_ps[:, :]).then_inc(sOGE.h, 1)
                    sOGE.n += 1

                def ot_batch(wb):
                    WG(sc, sRr.h, b["rr"] + wb + 1)
                    for cc in range(4):
                        j = 4 * wb + cc
                        WG(sc, sOGv.h, b["ogv"] + j - 3)
                        sc.activation(out=ot8[:, (j % 4) * 512:(j % 4 + 1) * 512],
                                      in_=o_win[:, (j % 6) * 512:(j % 6 + 1) * 512],
                                      func=AF.Identity,
                                      scale=G.rr_sb[:, (wb % 2) * 4 + cc:
                                                    (wb % 2) * 4 + cc + 1]
                                      ).then_inc(sOT.h, 1)
                        sOT.n += 1

                for c in range(NCH):
                    par = c % 2
                    w = c // 4
                    if c % 4 == 1 and c >= 5:
                        ot_batch(w - 1)
                    if c >= LAG + 1:
                        og_evac(c - LAG - 1)
                    WG(sc, sTk.h, b["tk"] + c + 1)
                    WG(sc, sTu.h, b["tu"] + 2 * c - 2)
                    sc.activation(out=ktok[:, par * 256:(par + 1) * 256],
                                  in_=kt_ps[:, par * 256:(par + 1) * 256],
                                  func=AF.Identity,
                                  scale=G.ck_sb[:, h:h + 1]).then_inc(sKT.h, 1)
                    sKT.n += 1
                    WG(sc, sTo.h, b["to"] + c + 1)
                    sc.copy(out=o_win[:, (c % 6) * 512:(c % 6 + 1) * 512],
                            in_=o_ps[:, par * 512:(par + 1) * 512]).then_inc(sOE.h, 1)
                    sOE.n += 1
                    WG(sc, sRr.h, b["rr"] + w - 1)
                    sc.activation(out=scr[:, 0:512],
                                  in_=o_ps[:, par * 512:(par + 1) * 512],
                                  func=AF.Square,
                                  accum_out=G.ss_sb[:, (w % 2) * 4 + (c % 4):
                                                    (w % 2) * 4 + (c % 4) + 1]
                                  ).then_inc(sOE.h, 1)
                    sOE.n += 1
                ot_batch(NCH // 4 - 1)
                for c in range(NCH - LAG - 1, NCH):
                    og_evac(c)

            @blk.sync
            def _(sy, h=h, b=b):
                for c in range(NCH):
                    WG(sy, sOGE.h, b["oge"] + c + 1)
                    sl = sSPs[c % 2]
                    sy.dma_start(out=G.og_sp[h * NCH + c, :, :],
                                 in_=og_st[:, (c % 2) * 512:(c % 2 + 1) * 512]
                                 ).then_inc(sl.h, 16)
                    sl.n += 16
                if h == H - 1:
                    WG(sy, sSPs[0].h, sSPs[0].n)
                    WG(sy, sSPs[1].h, sSPs[1].n)

            bprev = b

        @blk.gpsimd
        def _(g, bprev=bprev):
            emit_og_gp(g, bprev)


def _phase_wo_ffn(G, s):
    nc = G.nc
    sWO2, sWD, sTwo, sWoE = G.sWO2, G.sWD, G.sTwo, G.sWoE
    sOLs = [G.sOL0, G.sOL1]
    sSSs = [G.sSS0, G.sSS1]
    (sTg, sTup, sFE, sHT, sTd, sFE2, sCv) = (
        G.sTg, G.sTup, G.sFE, G.sHT, G.sTd, G.sFE2, G.sCv)
    sFWs = [G.sFW0, G.sFW1]
    sXFs = [G.sXF0, G.sXF1]
    sOUTs = [G.sOUT0, G.sOUT1]

    outer = ExitStack()
    SBo = lambda n, sh, dt: outer.enter_context(nc.sbuf_tensor(f"{n}{s}", sh, dt))
    PSo = lambda n, sh, dt: outer.enter_context(nc.psum_tensor(f"{n}{s}", sh, dt))
    with outer:
        wd_sb = SBo("wd_sb", [128, HM * D], BF16)
        wgu = SBo("wgu", [128, 4 * 1024], BF16)
        P0 = PSo("P0", [128, D], F32)
        P1 = PSo("P1", [128, D], F32)
        P2 = PSo("P2", [128, D], F32)
        P3 = PSo("P3", [128, D], F32)

        fw_pre = [sFWs[0].n, sFWs[1].n]

        inner1 = ExitStack()
        with inner1:
            wo_sb = inner1.enter_context(nc.sbuf_tensor(f"wo_sb{s}", [128, 16 * D], BF16))
            ogl = inner1.enter_context(nc.sbuf_tensor(f"ogl{s}", [128, 2 * 2048], BF16))
            s_st = inner1.enter_context(nc.sbuf_tensor(f"s_st{s}", [128, 2 * D], BF16))
            with nc.Block() as blk:
                wopss = (P0, P1)
                wl0, two0, we0 = sWO2.n, sTwo.n, sWoE.n
                ol0 = [sOLs[0].n, sOLs[1].n]
                ss0 = [sSSs[0].n, sSSs[1].n]

                sWGs = [G.sWG0, G.sWG1, G.sWG2, G.sWG3]
                xb = [t.n for t in sWGs]

                @blk.gpsimd
                def _(g):
                    for k in range(16):
                        sl = sWGs[k // 4]
                        g.dma_start(out=wo_sb[:, k * D:(k + 1) * D],
                                    in_=G.wo_d[k * 128:(k + 1) * 128, :]).then_inc(sl.h, 16)
                        sl.n += 16
                    for k in range(HM):
                        g.dma_start(out=wd_sb[:, k * D:(k + 1) * D],
                                    in_=G.fdw_d[k * 128:(k + 1) * 128, :]).then_inc(sWD.h, 16)
                    sWD.n += 16 * HM
                    for m in range(2):
                        WG(g, sTup.h, G.sTup.n + m - 1)
                        par = m % 2
                        sl = sFWs[par]
                        g.dma_start(out=wgu[:, par * 1024:(par + 1) * 1024]
                                    .rearrange("p (k j) -> p k j", j=128),
                                    in_=G.fgw_d[m].rearrange("(k p) j -> p k j", p=128)
                                    ).then_inc(sl.h, 16)
                        g.dma_start(out=wgu[:, 2048 + par * 1024:2048 + (par + 1) * 1024]
                                    .rearrange("p (k j) -> p k j", j=128),
                                    in_=G.fuw_d[m].rearrange("(k p) j -> p k j", p=128)
                                    ).then_inc(sl.h, 16)
                        sl.n += 32

                @blk.sync
                def _(sy):
                    for m in range(NCH + 2):
                        if m < NCH:
                            WG(sy, sTwo.h, two0 + m - 1)
                            sl = sOLs[m % 2]
                            for h in range(H):
                                sy.dma_start(
                                    out=ogl[:, (m % 2) * 2048 + h * 512:
                                            (m % 2) * 2048 + (h + 1) * 512],
                                    in_=G.og_sp[h * NCH + m, :, :]).then_inc(sl.h, 16)
                            sl.n += 64
                        if m >= 2:
                            ms = m - 2
                            WG(sy, sWoE.h, we0 + ms + 1)
                            sl = sSSs[ms % 2]
                            sy.dma_start(out=G.s_sp[ms * 128:(ms + 1) * 128, :],
                                         in_=s_st[:, (ms % 2) * D:(ms % 2 + 1) * D]
                                         ).then_inc(sl.h, 16)
                            sl.n += 16
                    WG(sy, sSSs[0].h, sSSs[0].n)
                    WG(sy, sSSs[1].h, sSSs[1].n)

                @blk.tensor
                def _(te):
                    for m in range(NCH):
                        WG(te, sOLs[m % 2].h, ol0[m % 2] + 64 * (m // 2 + 1))
                        WG(te, sWoE.h, we0 + m - 1)
                        ps = wopss[m % 2]
                        for k in range(16):
                            if m == 0 and k % 4 == 0:
                                te.wait_ge(sWGs[k // 4].h, xb[k // 4] + 64)
                            for n in range(2):
                                ins = te.matmul(
                                    ps[:, n * 512:(n + 1) * 512],
                                    ogl[:, (m % 2) * 2048 + k * 128:
                                        (m % 2) * 2048 + (k + 1) * 128],
                                    wo_sb[:, k * D + n * 512:k * D + (n + 1) * 512],
                                    start=(k == 0), stop=(k == 15))
                        sTwo.inc(ins)

                @blk.scalar
                def _(sc):
                    for m in range(NCH):
                        WG(sc, sTwo.h, two0 + m + 1)
                        WG(sc, sSSs[m % 2].h, ss0[m % 2] + 16 * (m // 2))
                        sc.copy(out=s_st[:, (m % 2) * D:(m % 2 + 1) * D],
                                in_=wopss[m % 2][:, :]).then_inc(sWoE.h, 1)
                        sWoE.n += 1
                we_end = sWoE.n

        inner2 = ExitStack()
        with inner2:
            SBi = lambda n, sh, dt: inner2.enter_context(nc.sbuf_tensor(f"{n}{s}", sh, dt))
            hT = SBi("hT", [128, HM * 1024], BF16)
            sg_sc = SBi("sg_sc", [128, 2 * 1024], BF16)
            g2_sc = SBi("g2_sc", [128, 2 * 1024], F32)
            xc_sb = SBi("xc_sb", [128, 2 * D], F32)
            sc_sb = SBi("sc_sb", [128, 2 * D], BF16)
            t_sb = SBi("t_sb", [128, 2 * D], F32)
            o_sb = SBi("o_sb", [128, 2 * D], F32)
            fgps, fups = P2, P3
            dpss = (P0, P1)
            with nc.Block() as blk:
                wl1 = sWD.n
                for half in range(2):
                    b = dict(fw=(fw_pre if half == 0 else [sFWs[0].n, sFWs[1].n]),
                             tg=sTg.n, tup=sTup.n,
                             fe=sFE.n, ht=sHT.n, td=sTd.n, fe2=sFE2.n, cv=sCv.n,
                             out=[sOUTs[0].n, sOUTs[1].n],
                             xl=[sXFs[0].n, sXFs[1].n])
                    tok0 = half * 1024

                    @blk.gpsimd
                    def _(g, half=half, b=b):
                        for m in range(2 if half == 0 else 0, HM):
                            WG(g, sTup.h, b["tup"] + m - 1)
                            par = m % 2
                            sl = sFWs[par]
                            g.dma_start(out=wgu[:, par * 1024:(par + 1) * 1024]
                                        .rearrange("p (k j) -> p k j", j=128),
                                        in_=G.fgw_d[m].rearrange("(k p) j -> p k j", p=128)
                                        ).then_inc(sl.h, 16)
                            g.dma_start(out=wgu[:, 2048 + par * 1024:
                                               2048 + (par + 1) * 1024]
                                        .rearrange("p (k j) -> p k j", j=128),
                                        in_=G.fuw_d[m].rearrange("(k p) j -> p k j", p=128)
                                        ).then_inc(sl.h, 16)
                            sl.n += 32

                    @blk.tensor
                    def _(te, half=half, b=b):
                        if half == 0:
                            WG(te, sWD.h, wl1)
                        for m in range(HM):
                            par = m % 2
                            WG(te, sFWs[par].h, b["fw"][par] + 32 * (m // 2 + 1))
                            WG(te, sFE.h, b["fe"] + m)
                            if half == 0 and m < 2:
                                pass
                            for n in range(2):
                                for k in range(KC):
                                    ins = te.matmul(
                                        fgps[:, n * 512:(n + 1) * 512],
                                        wgu[:, par * 1024 + k * 128:
                                            par * 1024 + (k + 1) * 128],
                                        G.xnT[:, k * T + tok0 + n * 512:
                                              k * T + tok0 + (n + 1) * 512],
                                        start=(k == 0), stop=(k == KC - 1))
                            sTg.inc(ins)
                            WG(te, sHT.h, b["ht"] + m)
                            for n in range(2):
                                for k in range(KC):
                                    ins = te.matmul(
                                        fups[:, n * 512:(n + 1) * 512],
                                        wgu[:, 2048 + par * 1024 + k * 128:
                                            2048 + par * 1024 + (k + 1) * 128],
                                        G.xnT[:, k * T + tok0 + n * 512:
                                              k * T + tok0 + (n + 1) * 512],
                                        start=(k == 0), stop=(k == KC - 1))
                            sTup.inc(ins)
                        WG(te, sHT.h, b["ht"] + HM)
                        for mc in range(8):
                            WG(te, sFE2.h, b["fe2"] + mc - 1)
                            if half == 0 and mc < 2:
                                WG(te, sWoE.h, we_end - 1 + mc)
                            ps = dpss[mc % 2]
                            for k in range(HM):
                                for n in range(2):
                                    ins = te.matmul(
                                        ps[:, n * 512:(n + 1) * 512],
                                        hT[:, k * 1024 + mc * 128:k * 1024 + (mc + 1) * 128],
                                        wd_sb[:, k * D + n * 512:k * D + (n + 1) * 512],
                                        start=(k == 0), stop=(k == HM - 1))
                            sTd.inc(ins)

                    @blk.scalar
                    def _(sc, half=half, b=b):
                        for m in range(HM):
                            WG(sc, sTg.h, b["tg"] + m + 1)
                            WG(sc, sHT.h, b["ht"] + m - 1)
                            sc.activation(out=sg_sc[:, (m % 2) * 1024:(m % 2 + 1) * 1024],
                                          in_=fgps[:, :], func=AF.Silu,
                                          scale=Q127).then_inc(sFE.h, 1)
                            sFE.n += 1
                        for mc in range(8):
                            WG(sc, sTd.h, b["td"] + mc + 1)
                            WG(sc, sCv.h, b["cv"] + mc - 1)
                            sc.activation(out=g2_sc[:, (mc % 2) * D:(mc % 2 + 1) * D],
                                          in_=dpss[mc % 2][:, :], func=AF.Silu,
                                          scale=Q127 * Q127).then_inc(sFE2.h, 1)
                            sFE2.n += 1

                    @blk.vector
                    def _(v, half=half, b=b):
                        for m in range(HM):
                            WG(v, sTup.h, b["tup"] + m + 1)
                            WG(v, sFE.h, b["fe"] + m + 1)
                            ins = v.tensor_mul(out=hT[:, m * 1024:(m + 1) * 1024],
                                               in0=sg_sc[:, (m % 2) * 1024:(m % 2 + 1) * 1024],
                                               in1=fups[:, :])
                            sHT.inc(ins)
                        for mc in range(8):
                            WG(v, sFE2.h, b["fe2"] + mc + 1)
                            WG(v, sXFs[mc % 2].h, b["xl"][mc % 2] + 32 * (mc // 2 + 1))
                            WG(v, sOUTs[mc % 2].h, b["out"][mc % 2] + 16 * (mc // 2))
                            v.tensor_mul(out=t_sb[:, (mc % 2) * D:(mc % 2 + 1) * D],
                                         in0=sc_sb[:, (mc % 2) * D:(mc % 2 + 1) * D],
                                         in1=g2_sc[:, (mc % 2) * D:(mc % 2 + 1) * D])
                            v.drain()
                            ins = v.tensor_add(out=o_sb[:, (mc % 2) * D:(mc % 2 + 1) * D],
                                               in0=t_sb[:, (mc % 2) * D:(mc % 2 + 1) * D],
                                               in1=xc_sb[:, (mc % 2) * D:(mc % 2 + 1) * D])
                            sCv.inc(ins)

                    @blk.sync
                    def _(sy, half=half, b=b):
                        for it in range(10):
                            if it < 8:
                                mc = it
                                row = s * T + tok0 + mc * 128
                                WG(sy, sCv.h, b["cv"] + mc - 1)
                                sl = sXFs[mc % 2]
                                sy.dma_start(out=xc_sb[:, (mc % 2) * D:(mc % 2 + 1) * D],
                                             in_=G.x_d[row:row + 128, :]).then_inc(sl.h, 16)
                                sy.dma_start(out=sc_sb[:, (mc % 2) * D:(mc % 2 + 1) * D],
                                             in_=G.s_sp[tok0 + mc * 128:
                                                        tok0 + mc * 128 + 128, :]
                                             ).then_inc(sl.h, 16)
                                sl.n += 32
                            if it >= 2:
                                mc = it - 2
                                row = s * T + tok0 + mc * 128
                                WG(sy, sCv.h, b["cv"] + mc + 1)
                                sl = sOUTs[mc % 2]
                                sy.dma_start(out=G.out_d[row:row + 128, :],
                                             in_=o_sb[:, (mc % 2) * D:(mc % 2 + 1) * D]
                                             ).then_inc(sl.h, 16)
                                sl.n += 16


_NC_CACHE = None


def kernel(L0_attn_x, L0_attn_Wq, L0_attn_Wk, L0_attn_Wv, L0_attn_Wg,
           L0_attn_Wo, L0_attn_gn_weight, L0_ffn_w_gate, L0_ffn_w_up,
           L0_ffn_w_down):
    global _NC_CACHE
    x = np.ascontiguousarray(np.asarray(L0_attn_x, dtype=np.float32))
    wq = np.ascontiguousarray(np.asarray(L0_attn_Wq, dtype=np.float32))
    wk = np.ascontiguousarray(np.asarray(L0_attn_Wk, dtype=np.float32))
    wv = np.ascontiguousarray(np.asarray(L0_attn_Wv, dtype=np.float32))
    wg = np.ascontiguousarray(np.asarray(L0_attn_Wg, dtype=np.float32))
    wo = np.ascontiguousarray(np.asarray(L0_attn_Wo, dtype=np.float32))
    gn = np.asarray(L0_attn_gn_weight, dtype=np.float32)
    gww = np.ascontiguousarray(np.broadcast_to(np.tile(gn, H)[None, :], (128, H * DV)))
    fg = np.asarray(L0_ffn_w_gate, dtype=np.int32)
    fu = np.asarray(L0_ffn_w_up, dtype=np.int32)
    fd = np.asarray(L0_ffn_w_down, dtype=np.int32)
    fgw = np.ascontiguousarray(fg.T.reshape(D, HM, 128).transpose(1, 0, 2))
    fuw = np.ascontiguousarray(fu.T.reshape(D, HM, 128).transpose(1, 0, 2))
    fdw = np.ascontiguousarray(fd.T)

    cosb, sinb, cqb, dpb, ckv, idn = _host_consts()

    if _NC_CACHE is None:
        _NC_CACHE = build_nc()
    nc = _NC_CACHE

    in_maps = []
    for c in range(NCORE):
        xc = np.ascontiguousarray(x[c * SEQ:(c + 1) * SEQ].reshape(SEQ * T, D))
        in_maps.append({
            "x": xc, "wq": wq, "wk": wk, "wv": wv, "wgt": wg, "wo": wo,
            "gww": gww, "fgw": fgw, "fuw": fuw, "fdw": fdw,
            "cosb": cosb, "sinb": sinb, "cqb": cqb, "dpb": dpb, "ckv": ckv,
            "idn": idn,
        })
    res = run_bass_kernel_spmd(nc, in_maps, core_ids=list(range(NCORE)))
    outs = [r["out"].reshape(SEQ, T, D) for r in res.results]
    return np.concatenate(outs, axis=0)


# version 15
# speedup vs baseline: 1.0611x; 1.0262x over previous
import numpy as np
import ml_dtypes
from contextlib import ExitStack
import concourse.bass as bass
import concourse.mybir as mybir
from concourse.bass_utils import run_bass_kernel_spmd

BF = ml_dtypes.bfloat16
F32 = mybir.dt.float32
BF16 = mybir.dt.bfloat16
I32 = mybir.dt.int32
AF = mybir.ActivationFunctionType
ALU = mybir.AluOpType
AX = mybir.AxisListType.X

NCORE = 8
SEQ = 2
T = 2048
D = 1024
H = 4
DV = 512
CH = 128
NCH = T // CH
KC = D // 128
HID = 3072
HM = HID // 128
EPS = 1e-5
MAGIC = 0x5F3759DF
Q127 = 1.0 / 127.0
LAG = 3

GAMMAS = [1.0 - 2.0 ** (-5.0 - h) for h in range(H)]

PHASES = ("ln", "attn", "wo", "ffn")
DEBUG = False
NHEADS = H
LN_STAGE = 7


def WG(eng, sem_h, val):
    if val > 0:
        eng.wait_ge(sem_h, val)


class Sem:
    def __init__(self, nc, name):
        self.h = nc.semaphore(name).__enter__()
        self.n = 0

    def inc(self, ins, k=1):
        ins.then_inc(self.h, k)
        self.n += k
        return self.n


def _host_consts():
    f = np.arange(128, dtype=np.float64)
    inv_freq = 1.0 / (10000.0 ** (2.0 * f / 256.0))
    t = np.arange(T, dtype=np.float64)
    ang = np.outer(inv_freq, t)
    cosb = np.cos(ang).astype(BF)
    sinb = np.sin(ang).astype(BF)
    tt = np.arange(512) % 128
    cq = np.zeros((128, H * 512), dtype=np.float64)
    for h in range(H):
        cq[:, h * 512:(h + 1) * 512] = (GAMMAS[h] ** (tt + 1.0)) / 16.0
    cqb = cq.astype(BF)
    j = np.arange(128)[:, None]
    i = np.arange(128)[None, :]
    dp = np.zeros((128, H * 128), dtype=np.float64)
    for h in range(H):
        dp[:, h * 128:(h + 1) * 128] = np.where(i >= j, GAMMAS[h] ** (-(j + 1.0)), 0.0)
    dpb = dp.astype(BF)
    ck = np.zeros((128, H), dtype=np.float64)
    for h in range(H):
        ck[:, h] = GAMMAS[h] ** (127.0 - np.arange(128))
    ckv = ck.astype(np.float32)
    idn = np.eye(128, dtype=BF)
    return cosb, sinb, cqb, dpb, ckv, idn


def _quake_rsqrt(v, out_ap, in_ap, qi, qf, sem=None):
    n = in_ap.shape[-1]
    v.tensor_single_scalar(out=qi[:, 0:n], in_=in_ap.bitcast(I32),
                           scalar=1, op=ALU.arith_shift_right)
    v.drain()
    v.tensor_scalar(out=qi[:, 0:n], in0=qi[:, 0:n], scalar1=MAGIC, scalar2=-1,
                    op0=ALU.subtract, op1=ALU.mult)
    v.drain()
    y = qi.ap().bitcast(F32)[:, 0:n]
    v.tensor_mul(out=qf[:, 0:n], in0=y, in1=y)
    v.drain()
    v.tensor_mul(out=qf[:, 0:n], in0=qf[:, 0:n], in1=in_ap)
    v.drain()
    v.tensor_scalar(out=qf[:, 0:n], in0=qf[:, 0:n], scalar1=-0.5, scalar2=1.5,
                    op0=ALU.mult, op1=ALU.add)
    v.drain()
    v.tensor_mul(out=out_ap, in0=y, in1=qf[:, 0:n])
    v.drain()
    v.tensor_mul(out=qf[:, 0:n], in0=out_ap, in1=out_ap)
    v.drain()
    v.tensor_mul(out=qf[:, 0:n], in0=qf[:, 0:n], in1=in_ap)
    v.drain()
    v.tensor_scalar(out=qf[:, 0:n], in0=qf[:, 0:n], scalar1=-0.5, scalar2=1.5,
                    op0=ALU.mult, op1=ALU.add)
    v.drain()
    ins = v.tensor_mul(out=out_ap, in0=out_ap, in1=qf[:, 0:n])
    if sem is not None:
        sem.inc(ins)
    v.drain()


def build_nc():
    nc = bass.Bass("TRN2", target_bir_lowering=False)

    x_d = nc.dram_tensor("x", [SEQ * T, D], F32, kind="ExternalInput")
    wq_d = nc.dram_tensor("wq", [D, D], F32, kind="ExternalInput")
    wk_d = nc.dram_tensor("wk", [D, D], F32, kind="ExternalInput")
    wv_d = nc.dram_tensor("wv", [D, 2048], F32, kind="ExternalInput")
    wgt_d = nc.dram_tensor("wgt", [D, 2048], F32, kind="ExternalInput")
    wo_d = nc.dram_tensor("wo", [2048, D], F32, kind="ExternalInput")
    gww_d = nc.dram_tensor("gww", [128, 2048], F32, kind="ExternalInput")
    fgw_d = nc.dram_tensor("fgw", [HM, D, 128], I32, kind="ExternalInput")
    fuw_d = nc.dram_tensor("fuw", [HM, D, 128], I32, kind="ExternalInput")
    fdw_d = nc.dram_tensor("fdw", [HID, D], I32, kind="ExternalInput")
    cos_d = nc.dram_tensor("cosb", [128, T], BF16, kind="ExternalInput")
    sin_d = nc.dram_tensor("sinb", [128, T], BF16, kind="ExternalInput")
    cq_d = nc.dram_tensor("cqb", [128, H * 512], BF16, kind="ExternalInput")
    dp_d = nc.dram_tensor("dpb", [128, H * 128], BF16, kind="ExternalInput")
    ck_d = nc.dram_tensor("ckv", [128, H], F32, kind="ExternalInput")
    id_d = nc.dram_tensor("idn", [128, 128], BF16, kind="ExternalInput")

    spill_kind = "ExternalOutput" if DEBUG else "Internal"
    og_sp = nc.dram_tensor("og_spill", [H * NCH, 128, 512], BF16, kind=spill_kind)
    s_sp = nc.dram_tensor("s_spill", [T, D], BF16, kind=spill_kind)
    out_d = nc.dram_tensor("out", [SEQ * T, D], F32, kind="ExternalOutput")
    dbg_xnT = (nc.dram_tensor("dbg_xnT", [128, KC * T], BF16, kind="ExternalOutput")
               if DEBUG else None)

    A = nc.alloc_sbuf_tensor
    cos_sb = A("cos_sb", [128, T], BF16)
    sin_sb = A("sin_sb", [128, T], BF16)
    cq_sb = A("cq_sb", [128, H * 512], BF16)
    dp_sb = A("dp_sb", [128, H * 128], BF16)
    ck_sb = A("ck_sb", [128, H], F32)
    id_sb = A("id_sb", [128, 128], BF16)
    gw_sb = A("gw_sb", [128, 2048], BF16)
    xnT = A("xnT", [128, KC * T], BF16)

    S0 = A("S0", [128, 1024], BF16)
    S1 = A("S1", [128, 1024], BF16)
    Sb = [S0, S1]
    ss_sb = A("ss_sb", [128, 8], F32)
    rr_sb = A("rr_sb", [128, 8], F32)
    ssA = A("ssA", [128, 64], F32)
    rrA = A("rrA", [128, 64], F32)
    qk1 = A("qk1", [128, 8], I32)
    qk2 = A("qk2", [128, 8], F32)
    qk3 = A("qk3", [128, 8], F32)
    lm_sb = A("lm_sb", [128, 8], F32)
    lr_sb = A("lr_sb", [128, 4], F32)
    lb_sb = A("lb_sb", [128, 4], F32)

    sems = {n: Sem(nc, n) for n in
            ("sWL sWQK sWO2 sWD sWH sLNq sLNv sLNs sTln sLNe "
             "sTp sPE sRP sGT sTa sTk sTo sTu sTog "
             "sAT sS sRr sOT sOGv sKT sOE sOGE "
             "sOL0 sOL1 sTwo sWoE sTd sTg sTup sFE sHT sFE2 sCv sCL "
             "sXL0 sXL1 sXL2 sXL3 sSP0 sSP1 sSS0 sSS1 "
             "sFW0 sFW1 sXF0 sXF1 sOUT0 sOUT1 "
             "sWG0 sWG1 sWG2 sWG3").split()}
    g = dict(sems)
    g.update(dict(nc=nc, x_d=x_d, wq_d=wq_d, wk_d=wk_d, wv_d=wv_d, wgt_d=wgt_d,
                  wo_d=wo_d, fgw_d=fgw_d, fuw_d=fuw_d, fdw_d=fdw_d,
                  og_sp=og_sp, s_sp=s_sp, out_d=out_d,
                  cos_sb=cos_sb, sin_sb=sin_sb, cq_sb=cq_sb, dp_sb=dp_sb,
                  ck_sb=ck_sb, id_sb=id_sb, gw_sb=gw_sb, xnT=xnT, Sb=Sb,
                  ss_sb=ss_sb, rr_sb=rr_sb, ssA=ssA, rrA=rrA,
                  qk1=qk1, qk2=qk2, qk3=qk3,
                  lm_sb=lm_sb, lr_sb=lr_sb, lb_sb=lb_sb))
    G = type("G", (), g)

    with nc.sbuf_tensor("gwf_sb", [128, 2048], F32) as gwf_sb, nc.Block() as blk:
        @blk.sync
        def _(sy):
            for dst, src in ((cos_sb, cos_d), (sin_sb, sin_d), (cq_sb, cq_d),
                             (dp_sb, dp_d), (ck_sb, ck_d), (id_sb, id_d),
                             (gwf_sb, gww_d)):
                sy.dma_start(out=dst[:, :], in_=src[:, :]).then_inc(G.sCL.h, 16)
            G.sCL.n += 7 * 16

        @blk.vector
        def _(v):
            WG(v, G.sCL.h, G.sCL.n)
            v.tensor_copy(out=gw_sb[:, :], in_=gwf_sb[:, :])

    nseq = 1 if DEBUG else SEQ
    for s in range(nseq):
        _phase_attn(G, s)
        _phase_wo_ffn(G, s)
    if DEBUG:
        with nc.Block() as blk:
            @blk.sync
            def _(sy):
                sy.dma_start(out=dbg_xnT[:, :], in_=xnT[:, :]).then_inc(G.sCL.h, 16)
                G.sCL.n += 16
                WG(sy, G.sCL.h, G.sCL.n)

    with nc.Block() as blk:
        @blk.sync
        def _(sy):
            WG(sy, G.sOUT0.h, G.sOUT0.n)
            WG(sy, G.sOUT1.h, G.sOUT1.n)

    return nc


def _emit_qk_evac(G, sc, b, eraw, U_ps, pair):
    WG(sc, G.sRP.h, b["rp"] + pair - 1)
    for d in range(2):
        grp = 2 * pair + d
        WG(sc, G.sTp.h, b["tp"] + grp + 1)
        par = grp % 2
        gp = pair % 2
        sc.copy(out=eraw[:, gp * 1024 + d * 512:gp * 1024 + (d + 1) * 512],
                in_=U_ps[:, par * 512:(par + 1) * 512]).then_inc(G.sPE.h, 1)
        G.sPE.n += 1


def _emit_ln(G, s, blk, x_sb, scr, xn_sb, lnps, extra_cb=None):
    nc = G.nc
    sLNq, sLNv, sLNs, sTln, sLNe = G.sLNq, G.sLNv, G.sLNs, G.sTln, G.sLNe
    sXLs = [G.sXL0, G.sXL1, G.sXL2, G.sXL3]
    x0s = [t.n for t in sXLs]
    q0, v0, s0, t0, e0 = sLNq.n, sLNv.n, sLNs.n, sTln.n, sLNe.n

    @blk.sync
    def _(sy):
        for tc in range(NCH):
            WG(sy, sLNs.h, s0 + tc - 3)
            WG(sy, sLNq.h, q0 + tc - 3)
            sl = sXLs[tc % 4]
            sy.dma_start(out=x_sb[:, (tc % 4) * D:(tc % 4 + 1) * D],
                         in_=G.x_d[s * T + tc * 128: s * T + (tc + 1) * 128, :]
                         ).then_inc(sl.h, 16)
            sl.n += 16

    @blk.vector
    def _(v):
        for w in range(4):
            for jj in range(4):
                tc = 4 * w + jj
                WG(v, sXLs[tc % 4].h, x0s[tc % 4] + 16 * (tc // 4 + 1))
                v.reduce_sum(out=G.lm_sb[:, jj:jj + 1],
                             in_=x_sb[:, (tc % 4) * D:(tc % 4 + 1) * D], axis=AX)
            WG(v, sLNq.h, q0 + 4 * (w + 1))
            v.drain()
            m1 = G.lm_sb[:, 0:4]
            m2 = G.lm_sb[:, 4:8]
            v.tensor_scalar_mul(out=G.qk2[:, 0:4], in0=m1, scalar1=1.0 / D)
            v.tensor_scalar(out=G.qk2[:, 4:8], in0=m2, scalar1=1.0 / D, scalar2=EPS,
                            op0=ALU.mult, op1=ALU.add)
            v.drain()
            v.tensor_mul(out=G.qk3[:, 0:4], in0=G.qk2[:, 0:4], in1=G.qk2[:, 0:4])
            v.drain()
            v.tensor_sub(out=G.qk3[:, 4:8], in0=G.qk2[:, 4:8], in1=G.qk3[:, 0:4])
            v.drain()
            _quake_rsqrt(v, G.lr_sb[:, 0:4], G.qk3[:, 4:8], G.qk1, G.qk2)
            ins = v.scalar_tensor_tensor(out=G.lb_sb[:, 0:4], in0=G.lm_sb[:, 0:4],
                                         scalar=-1.0 / D, in1=G.lr_sb[:, 0:4],
                                         op0=ALU.mult, op1=ALU.mult)
            sLNv.inc(ins)
            v.drain()
            for jj in range(4):
                tc = 4 * w + jj
                WG(v, sTln.h, t0 + tc - 3)
                ins = v.tensor_scalar(out=xn_sb[:, (tc % 4) * D:(tc % 4 + 1) * D],
                                      in0=x_sb[:, (tc % 4) * D:(tc % 4 + 1) * D],
                                      scalar1=G.lr_sb[:, jj:jj + 1],
                                      scalar2=G.lb_sb[:, jj:jj + 1],
                                      op0=ALU.mult, op1=ALU.add)
                sLNs.inc(ins)

    @blk.scalar
    def _(sc):
        def evac(tc):
            WG(sc, sTln.h, t0 + tc + 1)
            for k in range(KC):
                ins = sc.copy(
                    out=G.xnT[:, k * T + tc * 128: k * T + (tc + 1) * 128],
                    in_=lnps[:, k * 128:(k + 1) * 128])
            ins.then_inc(sLNe.h, 1)
            sLNe.n += 1

        for w in range(4):
            for jj in range(4):
                tc = 4 * w + jj
                WG(sc, sXLs[tc % 4].h, x0s[tc % 4] + 16 * (tc // 4 + 1))
                WG(sc, sLNv.h, v0 + w)
                sc.activation(out=scr[:, :],
                              in_=x_sb[:, (tc % 4) * D:(tc % 4 + 1) * D],
                              func=AF.Square,
                              accum_out=G.lm_sb[:, 4 + jj:5 + jj]
                              ).then_inc(sLNq.h, 1)
                sLNq.n += 1
                sc.drain()
            if w >= 1:
                for jj in range(4):
                    evac(4 * (w - 1) + jj)
                if extra_cb is not None:
                    extra_cb(sc, w)
        for tc in range(12, NCH):
            evac(tc)

    def ln_tr(te, tc):
        WG(te, sLNs.h, s0 + tc + 1)
        WG(te, sLNe.h, e0 + tc)
        for k in range(KC):
            ins = te.transpose(
                lnps[:, k * 128:(k + 1) * 128],
                xn_sb[:, (tc % 4) * D + k * 128:(tc % 4) * D + (k + 1) * 128],
                G.id_sb[:, :])
        ins.then_inc(G.sTln.h, 1)
        G.sTln.n += 1

    return e0, ln_tr


def _phase_attn(G, s):
    nc = G.nc
    (sWL, sWH, sTp, sPE, sRP, sGT, sTa, sTk, sTo, sTu, sTog, sAT, sS, sRr,
     sOT, sOGv, sKT, sOE, sOGE) = (G.sWL, G.sWH, G.sTp, G.sPE, G.sRP, G.sGT,
                                   G.sTa, G.sTk, G.sTo, G.sTu, G.sTog, G.sAT,
                                   G.sS, G.sRr, G.sOT, G.sOGv, G.sKT, G.sOE,
                                   G.sOGE)
    sSPs = [G.sSP0, G.sSP1]
    sLNe = G.sLNe
    sWQK = G.sWQK
    Sb = G.Sb
    es = ExitStack()
    SB = lambda n, sh, dt: es.enter_context(nc.sbuf_tensor(f"{n}{s}", sh, dt))
    PS = lambda n, sh, dt: es.enter_context(nc.psum_tensor(f"{n}{s}", sh, dt))
    wq_sb = SB("wq_sb", [128, KC * D], BF16)
    wk_sb = SB("wk_sb", [128, KC * D], BF16)
    wv_sb = SB("wv_sb", [128, KC * 512], BF16)
    wg_sb = SB("wg_sb", [128, KC * 512], BF16)
    qT = SB("qT", [128, 2 * T], BF16)
    kT = SB("kT", [128, 2 * T], BF16)
    vtb = SB("vtb", [128, NCH * 512], BF16)
    ggb = SB("ggb", [128, NCH * 512], BF16)
    ropem = SB("rope_m", [128, 6 * 512], BF16)
    eraw = SB("eraw", [128, 4 * 512], BF16)
    gtmp = SB("gtmp", [128, 2 * 512], BF16)
    atm = SB("atm", [128, 2 * 128], BF16)
    ktok = SB("ktok", [128, 2 * 256], BF16)
    o_win = SB("o_win", [128, 6 * 512], BF16)
    og_sb = SB("og_sb", [128, 4 * 512], BF16)
    og_st = SB("og_st", [128, 2 * 512], BF16)
    x_sb = SB("ln_x", [128, 4 * D], F32)
    scr = SB("ln_scr", [128, D], F32)
    xn_sb = SB("ln_xn", [128, 4 * D], BF16)
    U_ps = PS("U_ps", [128, 1024], F32)
    o_ps = PS("o_ps", [128, 2 * 512], F32)
    at_ps = PS("at_ps", [128, 2 * 128], F32)
    kt_ps = PS("kt_ps", [128, 2 * 256], BF16)
    ogT_ps = PS("ogT_ps", [128, 512], BF16)
    lnps = PS("lnps", [128, D], BF16)
    with es, nc.Block() as blk:
        wl0 = sWQK.n
        b_h0 = dict(tp=sTp.n, rp=sRP.n, pe=sPE.n)

        def _ln_extra(sc, w):
            _emit_qk_evac(G, sc, b_h0, eraw, U_ps, w - 1)

        ln_e0, ln_tr = _emit_ln(G, s, blk, x_sb, scr, xn_sb, lnps, _ln_extra)

        @blk.gpsimd
        def _(g):
            for k in range(KC):
                g.dma_start(out=wq_sb[:, k * D:(k + 1) * D],
                            in_=G.wq_d[k * 128:(k + 1) * 128, :]).then_inc(sWQK.h, 16)
                g.dma_start(out=wk_sb[:, k * D:(k + 1) * D],
                            in_=G.wk_d[k * 128:(k + 1) * 128, :]).then_inc(sWQK.h, 16)
            sWQK.n += 16 * 2 * KC

        def emit_og_gp(g, bo):
            for j in range(NCH):
                WG(g, sOE.h, bo["oe"] + 2 * j + 1)
                WG(g, sGT.h, bo["gt"] + j + 1)
                WG(g, sTog.h, bo["tog"] + j - 3)
                ins = g.tensor_mul(
                    out=og_sb[:, (j % 4) * 512:(j % 4 + 1) * 512],
                    in0=o_win[:, (j % 6) * 512:(j % 6 + 1) * 512],
                    in1=ggb[:, j * 512:(j + 1) * 512])
                sOGv.inc(ins)

        bprev = None
        for h in range(NHEADS):
            tp_gate = sTp.n

            @blk.gpsimd
            def _(g, h=h, tp_gate=tp_gate, bprev=bprev):
                WG(g, sTp.h, tp_gate)
                for k in range(KC):
                    g.dma_start(out=wv_sb[:, k * 512:(k + 1) * 512],
                                in_=G.wv_d[k * 128:(k + 1) * 128, h * 512:(h + 1) * 512]
                                ).then_inc(sWH.h, 16)
                    g.dma_start(out=wg_sb[:, k * 512:(k + 1) * 512],
                                in_=G.wgt_d[k * 128:(k + 1) * 128, h * 512:(h + 1) * 512]
                                ).then_inc(sWH.h, 16)
                sWH.n += 16 * 2 * KC
                if bprev is not None:
                    emit_og_gp(g, bprev)
            wl_h = sWH.n

            b = dict(tp=sTp.n, pe=(b_h0["pe"] if h == 0 else sPE.n), rp=sRP.n,
                     gt=sGT.n,
                     ta=sTa.n, tk=sTk.n, to=sTo.n, tu=sTu.n, tog=sTog.n,
                     at=sAT.n, st=sS.n, rr=sRr.n, ot=sOT.n, ogv=sOGv.n,
                     kt=sKT.n, oe=sOE.n, oge=sOGE.n,
                     sp=[sSPs[0].n, sSPs[1].n])

            @blk.tensor
            def _(te, h=h, b=b, wl_h=wl_h):
                if h == 0:
                    WG(te, sWQK.h, wl0 + 16 * 2 * KC)
                WG(te, sS.h, b["st"])
                grp = 0
                for wsb in (wq_sb, wk_sb):
                    for n in range(4):
                        if h == 0 and wsb is wq_sb:
                            for tc in range(4 * n, 4 * n + 4):
                                ln_tr(te, tc)
                        if h == 0:
                            WG(te, sLNe.h, ln_e0 + 4 * (n + 1))
                        for d in range(2):
                            par = grp % 2
                            WG(te, sPE.h, b["pe"] + grp - 1)
                            for k in range(KC):
                                ins = te.matmul(
                                    U_ps[:, par * 512:(par + 1) * 512],
                                    wsb[:, k * D + (2 * h + d) * 128:
                                        k * D + (2 * h + d + 1) * 128],
                                    G.xnT[:, k * T + n * 512:k * T + (n + 1) * 512],
                                    start=(k == 0), stop=(k == KC - 1))
                            sTp.inc(ins)
                            grp += 1
                WG(te, sWH.h, wl_h)
                for wsb in (wv_sb, wg_sb):
                    for c in range(NCH):
                        par = grp % 2
                        WG(te, sPE.h, b["pe"] + grp - 1)
                        for k in range(KC):
                            ins = te.matmul(
                                U_ps[:, par * 512:(par + 1) * 512],
                                G.xnT[:, k * T + c * 128:k * T + (c + 1) * 128],
                                wsb[:, k * 512:(k + 1) * 512],
                                start=(k == 0), stop=(k == KC - 1))
                        sTp.inc(ins)
                        grp += 1

            @blk.scalar
            def _(sc, h=h, b=b):
                WG(sc, sTo.h, b["to"])
                WG(sc, sTu.h, b["tu"])
                first_pair = 3 if h == 0 else 0
                for pair in range(first_pair, 8):
                    _emit_qk_evac(G, sc, b, eraw, U_ps, pair)
                grp = 16
                for c in range(NCH):
                    WG(sc, sTp.h, b["tp"] + grp + 1)
                    par = grp % 2
                    sc.copy(out=vtb[:, c * 512:(c + 1) * 512],
                            in_=U_ps[:, par * 512:(par + 1) * 512]).then_inc(sPE.h, 1)
                    sPE.n += 1
                    grp += 1
                for c in range(NCH):
                    WG(sc, sTp.h, b["tp"] + grp + 1)
                    WG(sc, sGT.h, b["gt"] + c - 1)
                    par = grp % 2
                    sc.activation(out=gtmp[:, (c % 2) * 512:(c % 2 + 1) * 512],
                                  in_=U_ps[:, par * 512:(par + 1) * 512],
                                  func=AF.Silu).then_inc(sPE.h, 1)
                    sPE.n += 1
                    grp += 1

            @blk.vector
            def _(v, h=h, b=b):
                WG(v, sTo.h, b["to"])
                WG(v, sTa.h, b["ta"])
                WG(v, sTk.h, b["tk"])
                v.memset(Sb[0][:, :], 0.0)
                for mi, (dst, has_cq) in enumerate(((qT, True), (kT, False))):
                    for n in range(4):
                        pair = mi * 4 + n
                        gp = pair % 2
                        WG(v, sPE.h, b["pe"] + 2 * pair + 2)
                        v.drain()
                        e0r = eraw[:, gp * 1024:gp * 1024 + 512]
                        e1r = eraw[:, gp * 1024 + 512:gp * 1024 + 1024]
                        e0 = dst[:, 0 * T + n * 512:0 * T + (n + 1) * 512]
                        e1 = dst[:, 1 * T + n * 512:1 * T + (n + 1) * 512]
                        c_sl = G.cos_sb[:, n * 512:(n + 1) * 512]
                        s_sl = G.sin_sb[:, n * 512:(n + 1) * 512]
                        p0 = ropem[:, 0:512]
                        p1 = ropem[:, 512:1024]
                        p2 = ropem[:, 1024:1536]
                        p3 = ropem[:, 1536:2048]
                        if has_cq:
                            cqs = G.cq_sb[:, h * 512:(h + 1) * 512]
                            a0 = ropem[:, 2048:2560]
                            a1 = ropem[:, 2560:3072]
                            v.tensor_mul(out=a0, in0=e0r, in1=cqs)
                            v.tensor_mul(out=a1, in0=e1r, in1=cqs)
                            v.drain()
                            e0r, e1r = a0, a1
                        v.tensor_mul(out=p0, in0=e0r, in1=c_sl)
                        v.tensor_mul(out=p1, in0=e1r, in1=s_sl)
                        v.tensor_mul(out=p2, in0=e0r, in1=s_sl)
                        v.tensor_mul(out=p3, in0=e1r, in1=c_sl)
                        v.drain()
                        v.tensor_sub(out=e0, in0=p0, in1=p1)
                        ins = v.tensor_add(out=e1, in0=p3, in1=p2)
                        sRP.inc(ins)
                for c in range(NCH):
                    WG(v, sPE.h, b["pe"] + 32 + c + 1)
                    WG(v, sOGv.h, b["ogv"] + c - 15)
                    ins = v.tensor_mul(out=ggb[:, c * 512:(c + 1) * 512],
                                       in0=gtmp[:, (c % 2) * 512:(c % 2 + 1) * 512],
                                       in1=G.gw_sb[:, h * 512:(h + 1) * 512])
                    sGT.inc(ins)

            @blk.tensor
            def _(te, h=h, b=b):
                WG(te, sRP.h, b["rp"] + 8)
                WG(te, sPE.h, b["pe"] + 32 + NCH)

                def og_tr(c):
                    WG(te, sOGv.h, b["ogv"] + c + 1)
                    WG(te, sOGE.h, b["oge"] + c)
                    for jj in range(4):
                        ins = te.transpose(
                            ogT_ps[:, jj * 128:(jj + 1) * 128],
                            og_sb[:, (c % 4) * 512 + jj * 128:
                                  (c % 4) * 512 + (jj + 1) * 128],
                            G.id_sb[:, :])
                    sTog.inc(ins)

                def at_kt(cc):
                    par2 = cc % 2
                    WG(te, sAT.h, b["at"] + cc)
                    for d in range(2):
                        ins = te.matmul(at_ps[:, par2 * 128:(par2 + 1) * 128],
                                        kT[:, d * T + cc * 128:d * T + (cc + 1) * 128],
                                        qT[:, d * T + cc * 128:d * T + (cc + 1) * 128],
                                        start=(d == 0), stop=(d == 1))
                    sTa.inc(ins)
                    WG(te, sKT.h, b["kt"] + cc)
                    for d in range(2):
                        ins = te.transpose(
                            kt_ps[:, par2 * 256 + d * 128:par2 * 256 + (d + 1) * 128],
                            kT[:, d * T + cc * 128:d * T + (cc + 1) * 128],
                            G.id_sb[:, :])
                    sTk.inc(ins)

                at_kt(0)
                at_kt(1)
                for c in range(NCH):
                    par = c % 2
                    if c >= LAG:
                        og_tr(c - LAG)
                    WG(te, sAT.h, b["at"] + c + 1)
                    WG(te, sOE.h, b["oe"] + 2 * c - 2)
                    WG(te, sS.h, b["st"] + 2 * c)
                    ins = te.matmul(o_ps[:, par * 512:(par + 1) * 512],
                                    atm[:, par * 128:(par + 1) * 128],
                                    vtb[:, c * 512:(c + 1) * 512],
                                    start=True, stop=(c == 0))
                    if c > 0:
                        for d in range(2):
                            ins = te.matmul(o_ps[:, par * 512:(par + 1) * 512],
                                            qT[:, d * T + c * 128:d * T + (c + 1) * 128],
                                            Sb[c % 2][:, d * 512:(d + 1) * 512],
                                            start=False, stop=(d == 1))
                    sTo.inc(ins)
                    WG(te, sKT.h, b["kt"] + c + 1)
                    for d in range(2):
                        WG(te, sS.h, b["st"] + 2 * c + d - 1)
                        ins = te.matmul(U_ps[:, d * 512:(d + 1) * 512],
                                        ktok[:, par * 256 + d * 128:
                                             par * 256 + (d + 1) * 128],
                                        vtb[:, c * 512:(c + 1) * 512],
                                        start=True, stop=True)
                        sTu.inc(ins)
                    if c + 2 < NCH:
                        at_kt(c + 2)
                for c in range(NCH - LAG, NCH):
                    og_tr(c)

            @blk.vector
            def _(v, h=h, b=b):
                def mask(cc):
                    par2 = cc % 2
                    WG(v, sTa.h, b["ta"] + cc + 1)
                    WG(v, sTo.h, b["to"] + cc - 1)
                    ins = v.tensor_mul(out=atm[:, par2 * 128:(par2 + 1) * 128],
                                       in0=at_ps[:, par2 * 128:(par2 + 1) * 128],
                                       in1=G.dp_sb[:, h * 128:(h + 1) * 128])
                    sAT.inc(ins)

                mask(0)
                mask(1)
                for c in range(NCH):
                    w = c // 4
                    if c < NCH - 1:
                        for d in range(2):
                            WG(v, sTu.h, b["tu"] + 2 * c + d + 1)
                            ins = v.scalar_tensor_tensor(
                                out=Sb[(c + 1) % 2][:, d * 512:(d + 1) * 512],
                                in0=Sb[c % 2][:, d * 512:(d + 1) * 512],
                                scalar=float(GAMMAS[h] ** CH),
                                in1=U_ps[:, d * 512:(d + 1) * 512],
                                op0=ALU.mult, op1=ALU.add)
                            sS.inc(ins)
                    if c + 2 < NCH:
                        mask(c + 2)
                WG(v, sOE.h, b["oe"] + 2 * NCH)
                for half in range(2):
                    sl0 = h * NCH + half * 8
                    v.tensor_scalar(out=G.qk3[:, 0:8],
                                    in0=G.ssA[:, sl0:sl0 + 8],
                                    scalar1=1.0 / 512.0, scalar2=EPS,
                                    op0=ALU.mult, op1=ALU.add)
                    v.drain()
                    _quake_rsqrt(v, G.rrA[:, sl0:sl0 + 8],
                                 G.qk3[:, 0:8], G.qk1, G.qk2, sem=sRr)

            @blk.scalar
            def _(sc, h=h, b=b):
                def og_evac(c):
                    WG(sc, sTog.h, b["tog"] + c + 1)
                    WG(sc, sSPs[c % 2].h, b["sp"][c % 2] + 16 * (c // 2))
                    sc.copy(out=og_st[:, (c % 2) * 512:(c % 2 + 1) * 512],
                            in_=ogT_ps[:, :]).then_inc(sOGE.h, 1)
                    sOGE.n += 1

                for c in range(NCH):
                    par = c % 2
                    w = c // 4
                    if c >= LAG + 1:
                        og_evac(c - LAG - 1)
                    WG(sc, sTk.h, b["tk"] + c + 1)
                    WG(sc, sTu.h, b["tu"] + 2 * c - 2)
                    sc.activation(out=ktok[:, par * 256:(par + 1) * 256],
                                  in_=kt_ps[:, par * 256:(par + 1) * 256],
                                  func=AF.Identity,
                                  scale=G.ck_sb[:, h:h + 1]).then_inc(sKT.h, 1)
                    sKT.n += 1
                    WG(sc, sTo.h, b["to"] + c + 1)
                    WG(sc, sOGv.h, b["ogv"] + c - 5)
                    sc.copy(out=o_win[:, (c % 6) * 512:(c % 6 + 1) * 512],
                            in_=o_ps[:, par * 512:(par + 1) * 512]).then_inc(sOE.h, 1)
                    sOE.n += 1
                    sc.activation(out=scr[:, 0:512],
                                  in_=o_ps[:, par * 512:(par + 1) * 512],
                                  func=AF.Square,
                                  accum_out=G.ssA[:, h * NCH + c:h * NCH + c + 1]
                                  ).then_inc(sOE.h, 1)
                    sOE.n += 1
                for c in range(NCH - LAG - 1, NCH):
                    og_evac(c)

            @blk.sync
            def _(sy, h=h, b=b):
                for c in range(NCH):
                    WG(sy, sOGE.h, b["oge"] + c + 1)
                    sl = sSPs[c % 2]
                    sy.dma_start(out=G.og_sp[h * NCH + c, :, :],
                                 in_=og_st[:, (c % 2) * 512:(c % 2 + 1) * 512]
                                 ).then_inc(sl.h, 16)
                    sl.n += 16
                if h == H - 1:
                    WG(sy, sSPs[0].h, sSPs[0].n)
                    WG(sy, sSPs[1].h, sSPs[1].n)

            bprev = b

        @blk.gpsimd
        def _(g, bprev=bprev):
            emit_og_gp(g, bprev)


def _phase_wo_ffn(G, s):
    nc = G.nc
    sWO2, sWD, sTwo, sWoE = G.sWO2, G.sWD, G.sTwo, G.sWoE
    sOLs = [G.sOL0, G.sOL1]
    sSSs = [G.sSS0, G.sSS1]
    (sTg, sTup, sFE, sHT, sTd, sFE2, sCv) = (
        G.sTg, G.sTup, G.sFE, G.sHT, G.sTd, G.sFE2, G.sCv)
    sFWs = [G.sFW0, G.sFW1]
    sXFs = [G.sXF0, G.sXF1]
    sOUTs = [G.sOUT0, G.sOUT1]

    outer = ExitStack()
    SBo = lambda n, sh, dt: outer.enter_context(nc.sbuf_tensor(f"{n}{s}", sh, dt))
    PSo = lambda n, sh, dt: outer.enter_context(nc.psum_tensor(f"{n}{s}", sh, dt))
    with outer:
        wd_sb = SBo("wd_sb", [128, HM * D], BF16)
        wgu = SBo("wgu", [128, 4 * 1024], BF16)
        P0 = PSo("P0", [128, D], F32)
        P1 = PSo("P1", [128, D], F32)
        P2 = PSo("P2", [128, D], F32)
        P3 = PSo("P3", [128, D], F32)

        fw_pre = [sFWs[0].n, sFWs[1].n]

        inner1 = ExitStack()
        with inner1:
            wo_sb = inner1.enter_context(nc.sbuf_tensor(f"wo_sb{s}", [128, 16 * D], BF16))
            ogl = inner1.enter_context(nc.sbuf_tensor(f"ogl{s}", [128, 2 * 2048], BF16))
            s_st = inner1.enter_context(nc.sbuf_tensor(f"s_st{s}", [128, 2 * D], BF16))
            rrb = inner1.enter_context(nc.sbuf_tensor(f"rrb{s}", [128, 128], BF16))
            rTf = inner1.enter_context(nc.sbuf_tensor(f"rTf{s}", [128, 128], BF16))
            rT_sb = inner1.enter_context(nc.sbuf_tensor(f"rT{s}", [16, H * 128], BF16))
            sel_sb = inner1.enter_context(nc.sbuf_tensor(f"sel{s}", [16, NCH * 128], BF16))
            ones_sb = inner1.enter_context(nc.sbuf_tensor(f"ones{s}", [1, 128], BF16))
            with nc.Block() as blk:
                wopss = (P0, P1)
                wl0, two0, we0 = sWO2.n, sTwo.n, sWoE.n
                r0, o0, td0, tg0 = G.sRr.n, G.sOT.n, sTd.n, sTg.n
                ol0 = [sOLs[0].n, sOLs[1].n]
                ss0 = [sSSs[0].n, sSSs[1].n]

                sWGs = [G.sWG0, G.sWG1, G.sWG2, G.sWG3]
                xb = [t.n for t in sWGs]

                @blk.gpsimd
                def _(g):
                    for k in range(16):
                        sl = sWGs[k // 4]
                        g.dma_start(out=wo_sb[:, k * D:(k + 1) * D],
                                    in_=G.wo_d[k * 128:(k + 1) * 128, :]).then_inc(sl.h, 16)
                        sl.n += 16
                    for k in range(HM):
                        g.dma_start(out=wd_sb[:, k * D:(k + 1) * D],
                                    in_=G.fdw_d[k * 128:(k + 1) * 128, :]).then_inc(sWD.h, 16)
                    sWD.n += 16 * HM
                    for m in range(2):
                        WG(g, sTup.h, G.sTup.n + m - 1)
                        par = m % 2
                        sl = sFWs[par]
                        g.dma_start(out=wgu[:, par * 1024:(par + 1) * 1024]
                                    .rearrange("p (k j) -> p k j", j=128),
                                    in_=G.fgw_d[m].rearrange("(k p) j -> p k j", p=128)
                                    ).then_inc(sl.h, 16)
                        g.dma_start(out=wgu[:, 2048 + par * 1024:2048 + (par + 1) * 1024]
                                    .rearrange("p (k j) -> p k j", j=128),
                                    in_=G.fuw_d[m].rearrange("(k p) j -> p k j", p=128)
                                    ).then_inc(sl.h, 16)
                        sl.n += 32

                @blk.sync
                def _(sy):
                    WG(sy, G.sOT.h, o0 + 1)
                    sy.dma_start_transpose(out=rTf[:, :],
                                           in_=rrb[:, :]).then_inc(sTg.h, 16)
                    sTg.n += 16
                    WG(sy, sTd.h, td0 + 1)
                    for mm in range(NCH):
                        sy.dma_start(out=sel_sb[mm:mm + 1, mm * 128:(mm + 1) * 128],
                                     in_=ones_sb[0:1, 0:128]).then_inc(sWO2.h, 16)
                    sWO2.n += 16 * NCH
                    WG(sy, sTg.h, tg0 + 16)
                    for hh in range(H):
                        sy.dma_start(out=rT_sb[0:16, hh * 128:(hh + 1) * 128],
                                     in_=rTf[hh * NCH:(hh + 1) * NCH, 0:128]
                                     ).then_inc(sWO2.h, 16)
                    sWO2.n += 16 * H
                    for m in range(NCH + 2):
                        if m < NCH:
                            WG(sy, sTwo.h, two0 + m - 1)
                            sl = sOLs[m % 2]
                            for h in range(H):
                                sy.dma_start(
                                    out=ogl[:, (m % 2) * 2048 + h * 512:
                                            (m % 2) * 2048 + (h + 1) * 512],
                                    in_=G.og_sp[h * NCH + m, :, :]).then_inc(sl.h, 16)
                            sl.n += 64
                        if m >= 2:
                            ms = m - 2
                            WG(sy, sWoE.h, we0 + ms + 1)
                            sl = sSSs[ms % 2]
                            sy.dma_start(out=G.s_sp[ms * 128:(ms + 1) * 128, :],
                                         in_=s_st[:, (ms % 2) * D:(ms % 2 + 1) * D]
                                         ).then_inc(sl.h, 16)
                            sl.n += 16
                    WG(sy, sSSs[0].h, sSSs[0].n)
                    WG(sy, sSSs[1].h, sSSs[1].n)

                @blk.tensor
                def _(te):
                    def rbt(m):
                        WG(te, G.sOT.h, o0 + 1 + m - 1)
                        for hh in range(H):
                            ins = te.matmul(
                                P2[:, (m % 2) * 512 + hh * 128:
                                   (m % 2) * 512 + (hh + 1) * 128],
                                sel_sb[0:16, m * 128:(m + 1) * 128],
                                rT_sb[0:16, hh * 128:(hh + 1) * 128],
                                start=True, stop=True)
                        G.sRr.inc(ins)

                    WG(te, sWO2.h, wl0 + 16 * 20)
                    rbt(0)
                    rbt(1)
                    for m in range(NCH):
                        WG(te, G.sOT.h, o0 + 1 + m + 1)
                        WG(te, sWoE.h, we0 + m - 1)
                        ps = wopss[m % 2]
                        for k in range(16):
                            if m == 0 and k % 4 == 0:
                                te.wait_ge(sWGs[k // 4].h, xb[k // 4] + 64)
                            for n in range(2):
                                ins = te.matmul(
                                    ps[:, n * 512:(n + 1) * 512],
                                    ogl[:, (m % 2) * 2048 + k * 128:
                                        (m % 2) * 2048 + (k + 1) * 128],
                                    wo_sb[:, k * D + n * 512:k * D + (n + 1) * 512],
                                    start=(k == 0), stop=(k == 15))
                        sTwo.inc(ins)
                        if m + 2 < NCH:
                            rbt(m + 2)

                @blk.vector
                def _(v):
                    v.memset(sel_sb[:, :], 0.0)
                    ins = v.memset(ones_sb[:, :], 1.0)
                    sTd.inc(ins)
                    for m in range(NCH):
                        WG(v, sOLs[m % 2].h, ol0[m % 2] + 64 * (m // 2 + 1))
                        WG(v, G.sRr.h, r0 + m + 1)
                        WG(v, sTwo.h, two0 + m - 1)
                        for k in range(16):
                            hh = k // 4
                            ins = v.tensor_mul(
                                out=ogl[:, (m % 2) * 2048 + k * 128:
                                        (m % 2) * 2048 + (k + 1) * 128],
                                in0=ogl[:, (m % 2) * 2048 + k * 128:
                                        (m % 2) * 2048 + (k + 1) * 128],
                                in1=P2[:, (m % 2) * 512 + hh * 128:
                                       (m % 2) * 512 + (hh + 1) * 128])
                        G.sOT.inc(ins)

                @blk.scalar
                def _(sc):
                    sc.copy(out=rrb[:, 64:128], in_=G.rrA[:, :])
                    sc.copy(out=rrb[:, 0:64], in_=G.rrA[:, :]).then_inc(G.sOT.h, 1)
                    G.sOT.n += 1
                    for m in range(NCH):
                        WG(sc, sTwo.h, two0 + m + 1)
                        WG(sc, sSSs[m % 2].h, ss0[m % 2] + 16 * (m // 2))
                        sc.copy(out=s_st[:, (m % 2) * D:(m % 2 + 1) * D],
                                in_=wopss[m % 2][:, :]).then_inc(sWoE.h, 1)
                        sWoE.n += 1
                we_end = sWoE.n

        inner2 = ExitStack()
        with inner2:
            SBi = lambda n, sh, dt: inner2.enter_context(nc.sbuf_tensor(f"{n}{s}", sh, dt))
            hT = SBi("hT", [128, HM * 1024], BF16)
            sg_sc = SBi("sg_sc", [128, 2 * 1024], BF16)
            g2_sc = SBi("g2_sc", [128, 2 * 1024], F32)
            xc_sb = SBi("xc_sb", [128, 2 * D], F32)
            sc_sb = SBi("sc_sb", [128, 2 * D], BF16)
            t_sb = SBi("t_sb", [128, 2 * D], F32)
            o_sb = SBi("o_sb", [128, 2 * D], F32)
            fgps, fups = P2, P3
            dpss = (P0, P1)
            with nc.Block() as blk:
                wl1 = sWD.n
                for half in range(2):
                    b = dict(fw=(fw_pre if half == 0 else [sFWs[0].n, sFWs[1].n]),
                             tg=sTg.n, tup=sTup.n,
                             fe=sFE.n, ht=sHT.n, td=sTd.n, fe2=sFE2.n, cv=sCv.n,
                             out=[sOUTs[0].n, sOUTs[1].n],
                             xl=[sXFs[0].n, sXFs[1].n])
                    tok0 = half * 1024

                    @blk.gpsimd
                    def _(g, half=half, b=b):
                        for m in range(2 if half == 0 else 0, HM):
                            WG(g, sTup.h, b["tup"] + m - 1)
                            par = m % 2
                            sl = sFWs[par]
                            g.dma_start(out=wgu[:, par * 1024:(par + 1) * 1024]
                                        .rearrange("p (k j) -> p k j", j=128),
                                        in_=G.fgw_d[m].rearrange("(k p) j -> p k j", p=128)
                                        ).then_inc(sl.h, 16)
                            g.dma_start(out=wgu[:, 2048 + par * 1024:
                                               2048 + (par + 1) * 1024]
                                        .rearrange("p (k j) -> p k j", j=128),
                                        in_=G.fuw_d[m].rearrange("(k p) j -> p k j", p=128)
                                        ).then_inc(sl.h, 16)
                            sl.n += 32

                    @blk.tensor
                    def _(te, half=half, b=b):
                        if half == 0:
                            WG(te, sWD.h, wl1)
                        for m in range(HM):
                            par = m % 2
                            WG(te, sFWs[par].h, b["fw"][par] + 32 * (m // 2 + 1))
                            WG(te, sFE.h, b["fe"] + m)
                            if half == 0 and m < 2:
                                pass
                            for n in range(2):
                                for k in range(KC):
                                    ins = te.matmul(
                                        fgps[:, n * 512:(n + 1) * 512],
                                        wgu[:, par * 1024 + k * 128:
                                            par * 1024 + (k + 1) * 128],
                                        G.xnT[:, k * T + tok0 + n * 512:
                                              k * T + tok0 + (n + 1) * 512],
                                        start=(k == 0), stop=(k == KC - 1))
                            sTg.inc(ins)
                            WG(te, sHT.h, b["ht"] + m)
                            for n in range(2):
                                for k in range(KC):
                                    ins = te.matmul(
                                        fups[:, n * 512:(n + 1) * 512],
                                        wgu[:, 2048 + par * 1024 + k * 128:
                                            2048 + par * 1024 + (k + 1) * 128],
                                        G.xnT[:, k * T + tok0 + n * 512:
                                              k * T + tok0 + (n + 1) * 512],
                                        start=(k == 0), stop=(k == KC - 1))
                            sTup.inc(ins)
                        WG(te, sHT.h, b["ht"] + HM)
                        for mc in range(8):
                            WG(te, sFE2.h, b["fe2"] + mc - 1)
                            if half == 0 and mc < 2:
                                WG(te, sWoE.h, we_end - 1 + mc)
                            ps = dpss[mc % 2]
                            for k in range(HM):
                                for n in range(2):
                                    ins = te.matmul(
                                        ps[:, n * 512:(n + 1) * 512],
                                        hT[:, k * 1024 + mc * 128:k * 1024 + (mc + 1) * 128],
                                        wd_sb[:, k * D + n * 512:k * D + (n + 1) * 512],
                                        start=(k == 0), stop=(k == HM - 1))
                            sTd.inc(ins)

                    @blk.scalar
                    def _(sc, half=half, b=b):
                        for m in range(HM):
                            WG(sc, sTg.h, b["tg"] + m + 1)
                            WG(sc, sHT.h, b["ht"] + m - 1)
                            sc.activation(out=sg_sc[:, (m % 2) * 1024:(m % 2 + 1) * 1024],
                                          in_=fgps[:, :], func=AF.Silu,
                                          scale=Q127).then_inc(sFE.h, 1)
                            sFE.n += 1
                        for mc in range(8):
                            WG(sc, sTd.h, b["td"] + mc + 1)
                            WG(sc, sCv.h, b["cv"] + mc - 1)
                            sc.activation(out=g2_sc[:, (mc % 2) * D:(mc % 2 + 1) * D],
                                          in_=dpss[mc % 2][:, :], func=AF.Silu,
                                          scale=Q127 * Q127).then_inc(sFE2.h, 1)
                            sFE2.n += 1

                    @blk.vector
                    def _(v, half=half, b=b):
                        for m in range(HM):
                            WG(v, sTup.h, b["tup"] + m + 1)
                            WG(v, sFE.h, b["fe"] + m + 1)
                            ins = v.tensor_mul(out=hT[:, m * 1024:(m + 1) * 1024],
                                               in0=sg_sc[:, (m % 2) * 1024:(m % 2 + 1) * 1024],
                                               in1=fups[:, :])
                            sHT.inc(ins)
                        for mc in range(8):
                            WG(v, sFE2.h, b["fe2"] + mc + 1)
                            WG(v, sXFs[mc % 2].h, b["xl"][mc % 2] + 32 * (mc // 2 + 1))
                            WG(v, sOUTs[mc % 2].h, b["out"][mc % 2] + 16 * (mc // 2))
                            v.tensor_mul(out=t_sb[:, (mc % 2) * D:(mc % 2 + 1) * D],
                                         in0=sc_sb[:, (mc % 2) * D:(mc % 2 + 1) * D],
                                         in1=g2_sc[:, (mc % 2) * D:(mc % 2 + 1) * D])
                            v.drain()
                            ins = v.tensor_add(out=o_sb[:, (mc % 2) * D:(mc % 2 + 1) * D],
                                               in0=t_sb[:, (mc % 2) * D:(mc % 2 + 1) * D],
                                               in1=xc_sb[:, (mc % 2) * D:(mc % 2 + 1) * D])
                            sCv.inc(ins)

                    @blk.sync
                    def _(sy, half=half, b=b):
                        for it in range(10):
                            if it < 8:
                                mc = it
                                row = s * T + tok0 + mc * 128
                                WG(sy, sCv.h, b["cv"] + mc - 1)
                                sl = sXFs[mc % 2]
                                sy.dma_start(out=xc_sb[:, (mc % 2) * D:(mc % 2 + 1) * D],
                                             in_=G.x_d[row:row + 128, :]).then_inc(sl.h, 16)
                                sy.dma_start(out=sc_sb[:, (mc % 2) * D:(mc % 2 + 1) * D],
                                             in_=G.s_sp[tok0 + mc * 128:
                                                        tok0 + mc * 128 + 128, :]
                                             ).then_inc(sl.h, 16)
                                sl.n += 32
                            if it >= 2:
                                mc = it - 2
                                row = s * T + tok0 + mc * 128
                                WG(sy, sCv.h, b["cv"] + mc + 1)
                                sl = sOUTs[mc % 2]
                                sy.dma_start(out=G.out_d[row:row + 128, :],
                                             in_=o_sb[:, (mc % 2) * D:(mc % 2 + 1) * D]
                                             ).then_inc(sl.h, 16)
                                sl.n += 16


_NC_CACHE = None


def kernel(L0_attn_x, L0_attn_Wq, L0_attn_Wk, L0_attn_Wv, L0_attn_Wg,
           L0_attn_Wo, L0_attn_gn_weight, L0_ffn_w_gate, L0_ffn_w_up,
           L0_ffn_w_down):
    global _NC_CACHE
    x = np.ascontiguousarray(np.asarray(L0_attn_x, dtype=np.float32))
    wq = np.ascontiguousarray(np.asarray(L0_attn_Wq, dtype=np.float32))
    wk = np.ascontiguousarray(np.asarray(L0_attn_Wk, dtype=np.float32))
    wv = np.ascontiguousarray(np.asarray(L0_attn_Wv, dtype=np.float32))
    wg = np.ascontiguousarray(np.asarray(L0_attn_Wg, dtype=np.float32))
    wo = np.ascontiguousarray(np.asarray(L0_attn_Wo, dtype=np.float32))
    gn = np.asarray(L0_attn_gn_weight, dtype=np.float32)
    gww = np.ascontiguousarray(np.broadcast_to(np.tile(gn, H)[None, :], (128, H * DV)))
    fg = np.asarray(L0_ffn_w_gate, dtype=np.int32)
    fu = np.asarray(L0_ffn_w_up, dtype=np.int32)
    fd = np.asarray(L0_ffn_w_down, dtype=np.int32)
    fgw = np.ascontiguousarray(fg.T.reshape(D, HM, 128).transpose(1, 0, 2))
    fuw = np.ascontiguousarray(fu.T.reshape(D, HM, 128).transpose(1, 0, 2))
    fdw = np.ascontiguousarray(fd.T)

    cosb, sinb, cqb, dpb, ckv, idn = _host_consts()

    if _NC_CACHE is None:
        _NC_CACHE = build_nc()
    nc = _NC_CACHE

    in_maps = []
    for c in range(NCORE):
        xc = np.ascontiguousarray(x[c * SEQ:(c + 1) * SEQ].reshape(SEQ * T, D))
        in_maps.append({
            "x": xc, "wq": wq, "wk": wk, "wv": wv, "wgt": wg, "wo": wo,
            "gww": gww, "fgw": fgw, "fuw": fuw, "fdw": fdw,
            "cosb": cosb, "sinb": sinb, "cqb": cqb, "dpb": dpb, "ckv": ckv,
            "idn": idn,
        })
    res = run_bass_kernel_spmd(nc, in_maps, core_ids=list(range(NCORE)))
    outs = [r["out"].reshape(SEQ, T, D) for r in res.results]
    return np.concatenate(outs, axis=0)
